# version 04
# baseline (speedup 1.0000x reference)
import numpy as np
import ml_dtypes
from contextlib import ExitStack

import concourse.bass as bass
import concourse.mybir as mybir
from concourse.bass_utils import run_bass_kernel_spmd

F32 = mybir.dt.float32
BF16 = mybir.dt.bfloat16
AF = mybir.ActivationFunctionType
ALU = mybir.AluOpType
AX = mybir.AxisListType.X

NB = 4
S = 2048
D = 1024
NDC = 8
NH = 16
HD = 64
NTB = 4
TBS = 512
NTC = 16
NE = 8
NI = 3584
NIBLK = 14
NIC = 28


def build_nc():
    nc = bass.Bass("TRN2", target_bir_lowering=False)

    x_fm = nc.dram_tensor("x_fm", [NB, D, S], F32, kind="ExternalInput").ap()
    wqk_t = nc.dram_tensor("wqk_t", [NH // 2, NDC, 128, 256], F32, kind="ExternalInput").ap()
    wv_t = nc.dram_tensor("wv_t", [NDC, 128, NH * HD], F32, kind="ExternalInput").ap()
    wout2d = nc.dram_tensor("wout2d", [D, D], F32, kind="ExternalInput").ap()
    rms_t = nc.dram_tensor("rms_t", [128, NDC], F32, kind="ExternalInput").ap()
    gw_t = nc.dram_tensor("gw_t", [NDC, 128, NE], F32, kind="ExternalInput").ap()
    w1_t = nc.dram_tensor("w1_t", [NE, NIBLK, NDC, 128, 256], F32, kind="ExternalInput").ap()
    w3_t = nc.dram_tensor("w3_t", [NE, NIBLK, NDC, 128, 256], F32, kind="ExternalInput").ap()
    w2_t = nc.dram_tensor("w2_t", [NE, NDC, NIC, 128, 128], F32, kind="ExternalInput").ap()
    idn_in = nc.dram_tensor("idn", [128, 128], F32, kind="ExternalInput").ap()
    onc_in = nc.dram_tensor("onc", [128, 1], F32, kind="ExternalInput").ap()
    oncb_in = nc.dram_tensor("oncb", [128, 1], BF16, kind="ExternalInput").ap()
    onr_in = nc.dram_tensor("onr", [1, 128], F32, kind="ExternalInput").ap()
    mask_in = nc.dram_tensor("mask", [128, 896], BF16, kind="ExternalInput").ap()
    eps_in = nc.dram_tensor("epsc", [128, 2], F32, kind="ExternalInput").ap()
    out_fm = nc.dram_tensor("out_fm", [NB, D, S], F32, kind="ExternalOutput").ap()

    w1_s = nc.dram_tensor("w1_s", [NE, NIBLK, NDC, 128, 256], BF16, kind="Internal").ap()
    w3_s = nc.dram_tensor("w3_s", [NE, NIBLK, NDC, 128, 256], BF16, kind="Internal").ap()
    w2_s = nc.dram_tensor("w2_s", [NE, NDC, NIC, 128, 128], BF16, kind="Internal").ap()
    wqk_s = nc.dram_tensor("wqk_s", [NH // 2, NDC, 128, 256], BF16, kind="Internal").ap()
    wv_s = nc.dram_tensor("wv_s", [NDC, 128, NH * HD], BF16, kind="Internal").ap()
    scr = nc.dram_tensor("scr", [NH, S, HD], BF16, kind="Internal").ap()

    with ExitStack() as stack:
        e_ = stack.enter_context

        A_xnh = e_(nc.sbuf_tensor("A_xnh", [128, 16384], BF16))
        A_vact = e_(nc.sbuf_tensor("A_vact", [128, 16640], BF16))
        A_w13 = e_(nc.sbuf_tensor("A_w13", [128, 8192], BF16))
        A_wsm = e_(nc.sbuf_tensor("A_wsm", [128, 8192], BF16))
        wqkh = A_w13
        q_sb = e_(nc.sbuf_tensor("q_sb", [128, 2048], BF16))
        k_sb = e_(nc.sbuf_tensor("k_sb", [128, 2048], BF16))
        E_sb = e_(nc.sbuf_tensor("E_sb", [128, 3072], BF16))
        sil_sb = e_(nc.sbuf_tensor("sil_sb", [128, 1024], BF16))
        xattn = e_(nc.sbuf_tensor("xattn", [128, 1024], BF16))
        outsb = e_(nc.sbuf_tensor("outsb", [128, 512], BF16))
        mask_sb = e_(nc.sbuf_tensor("mask_sb", [128, 896], BF16))
        oncb = e_(nc.sbuf_tensor("oncb_sb", [128, 1], BF16))
        xsqr = E_sb
        xbfr = sil_sb

        x2acc = e_(nc.sbuf_tensor("x2acc", [128, 16384], F32))
        xring = e_(nc.sbuf_tensor("xring", [128, 1024], F32))
        tmp2 = e_(nc.sbuf_tensor("tmp2", [128, 512], F32))
        bc_sb = e_(nc.sbuf_tensor("bc_sb", [128, 512], F32))
        rows = e_(nc.sbuf_tensor("rows", [1, 2048], F32))

        cw_all = e_(nc.sbuf_tensor("cw_all", [128, 32], F32))
        gl8f = e_(nc.sbuf_tensor("gl8f", [8, 1024], F32))
        gl8 = gl8f[:, 0:512]
        cwT_e = gl8f[0:1, 0:1024]
        s_sb = e_(nc.sbuf_tensor("s_sb", [8, 1], F32))
        srow = e_(nc.sbuf_tensor("srow", [1, 8], F32))
        rtv = e_(nc.sbuf_tensor("rtv", [128, 240], F32))
        rz_sb = e_(nc.sbuf_tensor("rz_sb", [128, 8], F32))
        wsum_c = e_(nc.sbuf_tensor("wsum_c", [128, 8], F32))
        rms_c = e_(nc.sbuf_tensor("rms_c", [128, 8], F32))
        gw_sb = e_(nc.sbuf_tensor("gw_sb", [128, 64], F32))
        gwp = e_(nc.sbuf_tensor("gwp", [128, 64], F32))
        sbc = e_(nc.sbuf_tensor("sbc", [128, 8], F32))
        idn_sb = e_(nc.sbuf_tensor("idn_sb", [128, 128], F32))
        onc = e_(nc.sbuf_tensor("onc_sb", [128, 1], F32))
        onr = e_(nc.sbuf_tensor("onr_sb", [1, 128], F32))
        epsc = e_(nc.sbuf_tensor("epsc_sb", [128, 2], F32))
        nc.const_aps.register(F32, 1e-5, epsc[:, 0:1])
        nc.const_aps.register(F32, 1e-6, epsc[:, 1:2])

        psA = e_(nc.psum_tensor("psA", [128, 1024], F32))
        psB = e_(nc.psum_tensor("psB", [128, 1024], F32))
        psC = e_(nc.psum_tensor("psC", [128, 1024], F32))
        psD = e_(nc.psum_tensor("psD", [128, 1024], F32))

        class Sem:
            def __init__(self, name):
                self.h = e_(nc.semaphore(name))
                self.n = 0

        SN = ("s_const s_cin0 s_cin1 s_castD s_castA s_cout0 s_cout1 s_wsmm "
              "s_pe s_row s_stat s_bcm "
              "s_xld0 s_xld1 s_sq s_xbf s_xn s_wv s_wqk s_qkmm s_qkev s_vmm s_vev "
              "s_scm s_exp s_msk s_pvmm s_pvev s_scr0 s_scr1 s_rb0 s_rb1 "
              "s_x2 s_gmm s_f1 "
              "s_hrdy s_acc s_cwt s_bcw s_w13a s_w13b s_w2a s_w2b "
              "s_pa s_pg s_peo s_sil s_mul "
              "s_drn s_bcc s_out s_bar").split()
        sm = type("SM", (), {n: Sem(n) for n in SN})
        s_cin = [sm.s_cin0, sm.s_cin1]
        s_cout = [sm.s_cout0, sm.s_cout1]
        s_xld = [sm.s_xld0, sm.s_xld1]
        s_scr = [sm.s_scr0, sm.s_scr1]
        s_rb = [sm.s_rb0, sm.s_rb1]
        s_w13 = [sm.s_w13a, sm.s_w13b]
        s_w2 = [sm.s_w2a, sm.s_w2b]

        block = e_(nc.Block())
        ops = {q: [] for q in ("pe", "act", "dve", "sync", "pool")}

        selfsem = {"dve": Sem("self_dve"), "act": Sem("self_act")}
        last_done = {"dve": None, "act": None}

        def wt(q, s, v=None):
            v = s.n if v is None else v
            ops[q].append(lambda eng, h=s.h, vv=v: eng.wait_ge(h, vv))

        def em(q, meth, *args, inc=None, amt=1, chain=True, **kw):
            if q in ("dve", "act"):
                if chain and last_done[q] is not None:
                    ls, lv = last_done[q]
                    wt(q, ls, lv)
                if inc is None:
                    inc = selfsem[q]
                    amt = 1
            if inc is not None:
                inc.n += amt
            def f(eng, meth=meth, args=args, kw=kw, inc=inc, amt=amt):
                i = getattr(eng, meth)(*args, **kw)
                if inc is not None:
                    i.then_inc(inc.h, amt)
            ops[q].append(f)
            if q in ("dve", "act"):
                last_done[q] = (inc, inc.n)

        class Ring:
            def __init__(self, nslots):
                self.g = [[] for _ in range(nslots)]

            def acq(self, q, slot):
                for s, v in self.g[slot]:
                    wt(q, s, v)
                self.g[slot] = []

            def rel(self, slot, s, v=None):
                self.g[slot].append((s, s.n if v is None else v))

        pr = Ring(8)
        xr = Ring(2)

        wqk_r = Ring(1)
        E_r = Ring(6)
        sil_r = Ring(2)
        out_r = Ring(2)
        xat_r = Ring(2)
        w13_r = Ring(2)
        w2_r = Ring(2)
        cv_i = Ring(2)
        cv_o = Ring(2)

        def barrier():
            for q in ops:
                em(q, "nop", inc=sm.s_bar)
            v = sm.s_bar.n
            for q in ops:
                wt(q, sm.s_bar, v)
            last_done["dve"] = (sm.s_bar, v)
            last_done["act"] = (sm.s_bar, v)

        PS = {"A0": psA[:, 0:512], "A1": psA[:, 512:1024],
              "B0": psB[:, 0:512], "B1": psB[:, 512:1024],
              "C0": psC[:, 0:512], "C1": psC[:, 512:1024],
              "D0": psD[:, 0:512], "D1": psD[:, 512:1024]}
        BI = {"A0": 0, "A1": 1, "B0": 2, "B1": 3, "C0": 4, "C1": 5, "D0": 6, "D1": 7}

        def xn_sl(dc, t0, w=TBS):
            return A_xnh[:, dc * 2048 + t0: dc * 2048 + t0 + w]

        def x2_sl(dc, t0, w=TBS):
            return x2acc[:, dc * 2048 + t0: dc * 2048 + t0 + w]

        RW = [rows[0:1, 512 * k:512 * (k + 1)] for k in range(4)]

        em("sync", "dma_start", out=idn_sb[:, :], in_=idn_in, inc=sm.s_const, amt=16)
        em("sync", "dma_start", out=onc[:, :], in_=onc_in, inc=sm.s_const, amt=16)
        em("sync", "dma_start", out=oncb[:, :], in_=oncb_in, inc=sm.s_const, amt=16)
        em("sync", "dma_start", out=onr[:, :], in_=onr_in, inc=sm.s_const, amt=16)
        em("sync", "dma_start", out=mask_sb[:, :], in_=mask_in, inc=sm.s_const, amt=16)
        em("sync", "dma_start", out=rms_c[:, :], in_=rms_t, inc=sm.s_const, amt=16)
        em("sync", "dma_start", out=gw_sb[:, :].rearrange("p (c e) -> p c e", c=NDC),
           in_=gw_t.rearrange("c p e -> p c e"), inc=sm.s_const, amt=16)
        em("sync", "dma_start", out=epsc[:, :], in_=eps_in, inc=sm.s_const, amt=16)
        n_const = sm.s_const.n
        wt("act", sm.s_const, n_const)
        wt("dve", sm.s_const, n_const)

        cin = [x2acc[:, 0:4096], x2acc[:, 4096:8192]]
        cout = [A_xnh[:, 0:4096], A_xnh[:, 4096:8192]]

        wt("sync", sm.s_const, n_const)
        for i in range(8):
            cv_i.acq("sync", i % 2)
            em("sync", "dma_start", out=cin[i % 2][:, 0:1024],
               in_=wout2d[128 * i:128 * (i + 1), :], inc=s_cin[i % 2], amt=16)
            sld = s_cin[i % 2].n
            wt("pe", s_cin[i % 2], sld)
            if i == 0:
                pr.acq("pe", BI["D0"]); pr.acq("pe", BI["D1"])
            em("pe", "matmul", psD[0:1, 0:512], onc[:, :], cin[i % 2][:, 0:512],
               start=(i == 0), stop=(i == 7), skip_group_check=True)
            em("pe", "matmul", psD[0:1, 512:1024], onc[:, :], cin[i % 2][:, 512:1024],
               start=(i == 0), stop=(i == 7), skip_group_check=True, inc=sm.s_wsmm)
            cv_i.rel(i % 2, sm.s_wsmm)
        wt("dve", sm.s_wsmm, sm.s_wsmm.n)
        em("dve", "tensor_copy", rows[0:1, 0:512], psD[0:1, 0:512])
        em("dve", "tensor_copy", rows[0:1, 512:1024], psD[0:1, 512:1024], inc=sm.s_row)
        pr.rel(BI["D0"], sm.s_row); pr.rel(BI["D1"], sm.s_row)
        wsrow = sm.s_row.n
        for k in range(8):
            wt("pe", sm.s_row, wsrow + k)
            if k == 0:
                wt("pe", sm.s_const, n_const)
                pr.acq("pe", BI["B0"])
            em("pe", "transpose", psB[:, 0:1], rows[0:1, 128 * k:128 * (k + 1)],
               idn_sb[0:1, 0:1], inc=sm.s_pe)
            wt("dve", sm.s_pe, sm.s_pe.n)
            em("dve", "tensor_copy", wsum_c[:, k:k + 1], psB[:, 0:1], inc=sm.s_row)
        wt("dve", sm.s_const, n_const)
        for dc in range(NDC):
            em("dve", "tensor_scalar", gwp[:, 8 * dc:8 * dc + 8], gw_sb[:, 8 * dc:8 * dc + 8],
               rms_c[:, dc:dc + 1], None, ALU.mult,
               inc=(sm.s_row if dc == 7 else None))
        wt("pe", sm.s_row, sm.s_row.n)
        for dc in range(NDC):
            em("pe", "matmul", psB[0:8, 0:1], gwp[:, 8 * dc:8 * dc + 8], onc[:, :],
               start=(dc == 0), stop=(dc == 7),
               inc=(sm.s_pe if dc == 7 else None))
        wt("dve", sm.s_pe, sm.s_pe.n)
        em("dve", "tensor_copy", s_sb[:, :], psB[0:8, 0:1], inc=sm.s_row)
        wt("pe", sm.s_row, sm.s_row.n)
        em("pe", "transpose", psB[0:1, 0:8], s_sb[:, :], idn_sb[0:8, 0:8], inc=sm.s_pe)
        wt("dve", sm.s_pe, sm.s_pe.n)
        em("dve", "tensor_copy", srow[:, :], psB[0:1, 0:8], inc=sm.s_row)
        wt("pe", sm.s_row, sm.s_row.n)
        em("pe", "matmul", psB[:, 0:8], onr[:, :], srow[:, :], start=True, stop=True,
           inc=sm.s_pe)
        wt("dve", sm.s_pe, sm.s_pe.n)
        em("dve", "tensor_copy", sbc[:, :], psB[:, 0:8], inc=sm.s_row)
        pr.rel(BI["B0"], sm.s_row)

        tiles = []
        for e in range(NE):
            for ib in range(NIBLK):
                tiles.append((w1_t[e, ib].rearrange("c p i -> p c i"),
                              w1_s[e, ib].rearrange("c p i -> p c i"), 2048, NDC))
                tiles.append((w3_t[e, ib].rearrange("c p i -> p c i"),
                              w3_s[e, ib].rearrange("c p i -> p c i"), 2048, NDC))
        for e in range(NE):
            for dc in range(NDC):
                tiles.append((w2_t[e, dc].rearrange("c p d -> p c d"),
                              w2_s[e, dc].rearrange("c p d -> p c d"), 3584, NIC))
        for n in range(NH // 2):
            tiles.append((wqk_t[n].rearrange("c p v -> p c v"),
                          wqk_s[n].rearrange("c p v -> p c v"), 2048, NDC))
        for dc in range(NDC):
            tiles.append((wv_t[dc], wv_s[dc], 1024, None))

        cast_sem = [sm.s_castD, sm.s_castA]
        cast_q = ["dve", "act"]
        cast_meth = ["tensor_copy", "copy"]
        for i, (src, dst, ncol, csplit) in enumerate(tiles):
            sl = i % 2
            cin_v = cin[sl][:, 0:ncol]
            cout_v = cout[sl][:, 0:ncol]
            if csplit is not None:
                cin_v = cin_v.rearrange("p (c v) -> p c v", c=csplit)
                cout_v = cout_v.rearrange("p (c v) -> p c v", c=csplit)
            cv_i.acq("sync", sl)
            em("sync", "dma_start", out=cin_v, in_=src, inc=s_cin[sl], amt=16)
            sld = s_cin[sl].n
            q = cast_q[sl]
            wt(q, s_cin[sl], sld)
            cv_o.acq(q, sl)
            em(q, cast_meth[sl], cout[sl][:, 0:ncol], cin[sl][:, 0:ncol],
               inc=cast_sem[sl])
            cv_i.rel(sl, cast_sem[sl])
            scast = cast_sem[sl].n
            wt("pool", cast_sem[sl], scast)
            em("pool", "dma_start", out=dst, in_=cout_v, inc=s_cout[sl], amt=16)
            cv_o.rel(sl, s_cout[sl])

        for q in ("sync", "pool"):
            wt(q, s_cout[0], s_cout[0].n)
            wt(q, s_cout[1], s_cout[1].n)
        barrier()

        jg = 0
        Qg = 0
        exp_hist = {}
        pvmm_hist = {}
        pvev_hist = {}
        scr_hist = {}
        vev_hist = {}
        slab_used = {}
        w2_used = {}
        gi_sil = {}
        gi_mul = {}
        x2_hist = {}

        for b in range(NB):
            em("sync", "dma_start",
               out=A_wsm[:, 0:8192].rearrange("p (c v) -> p c v", c=NDC),
               in_=wv_s.rearrange("c p v -> p c v"), inc=sm.s_wv, amt=16)
            snap_wv = sm.s_wv.n
            em("pool", "memset", A_vact[:, :], 1.0, inc=sm.s_msk)
            snap_msv = sm.s_msk.n

            snap_xn = {}
            a1_state = {}

            def a1_pass1(tb):
                t0 = tb * TBS
                rb = ("D0", "D1") if tb % 2 == 0 else ("A0", "A1")
                psR = psD if tb % 2 == 0 else psA
                for dc in range(NDC):
                    i = tb * NDC + dc
                    sl = i % 2
                    xr.acq("sync", sl)
                    em("sync", "dma_start", out=xring[:, 512 * sl:512 * (sl + 1)],
                       in_=x_fm[b, dc * 128:(dc + 1) * 128, t0:t0 + TBS],
                       inc=s_xld[sl], amt=16)
                    sxl = s_xld[sl].n
                    wt("act", s_xld[sl], sxl)
                    E_r.acq("act", sl)
                    em("act", "activation", xsqr[:, 512 * sl:512 * (sl + 1)],
                       xring[:, 512 * sl:512 * (sl + 1)], AF.Square, inc=sm.s_sq,
                       chain=False)
                    ssq = sm.s_sq.n
                    wt("dve", s_xld[sl], sxl)
                    em("dve", "tensor_copy", xn_sl(dc, t0),
                       xring[:, 512 * sl:512 * (sl + 1)], inc=sm.s_xbf, chain=False)
                    sbf = sm.s_xbf.n
                    xr.rel(sl, sm.s_sq, ssq)
                    xr.rel(sl, sm.s_xbf, sbf)
                    wt("pe", sm.s_xbf, sbf)
                    wt("pe", sm.s_sq, ssq)
                    if dc == 0:
                        pr.acq("pe", BI[rb[0]]); pr.acq("pe", BI[rb[1]])
                    em("pe", "matmul", psR[0:1, 0:512], oncb[:, :],
                       xn_sl(dc, t0),
                       start=(dc == 0), stop=(dc == 7), skip_group_check=True)
                    em("pe", "matmul", psR[0:1, 512:1024], oncb[:, :],
                       xsqr[:, 512 * sl:512 * (sl + 1)],
                       start=(dc == 0), stop=(dc == 7), skip_group_check=True,
                       inc=sm.s_stat)
                    E_r.rel(sl, sm.s_stat)
                a1_state[tb] = sm.s_stat.n

            def a1_pass2(tb):
                t0 = tb * TBS
                rb = ("D0", "D1") if tb % 2 == 0 else ("A0", "A1")
                psR = psD if tb % 2 == 0 else psA
                bb = ("B0", "B1") if tb % 2 == 0 else ("C0", "C1")
                psBC = psB if tb % 2 == 0 else psC
                wt("dve", sm.s_stat, a1_state[tb])
                wt("dve", sm.s_bcm, sm.s_bcm.n)
                em("dve", "tensor_scalar", RW[0], psR[0:1, 0:512], 1.0 / 1024, None, ALU.mult)
                em("dve", "tensor_scalar", RW[1], psR[0:1, 512:1024], 1.0 / 1024, None, ALU.mult)
                em("dve", "tensor_tensor", RW[2], RW[0], RW[0], ALU.mult)
                em("dve", "tensor_tensor", RW[1], RW[1], RW[2], ALU.subtract, inc=sm.s_row)
                pr.rel(BI[rb[0]], sm.s_row); pr.rel(BI[rb[1]], sm.s_row)
                wt("act", sm.s_row, sm.s_row.n)
                em("act", "activation", RW[2], RW[1], AF.Sqrt, bias=1e-5, inc=sm.s_sil)
                wt("dve", sm.s_sil, sm.s_sil.n)
                em("dve", "reciprocal", RW[1], RW[2])
                em("dve", "tensor_tensor", RW[2], RW[0], RW[1], ALU.mult)
                em("dve", "tensor_scalar", RW[2], RW[2], -1.0, None, ALU.mult, inc=sm.s_row)
                snap_row = sm.s_row.n
                wt("pe", sm.s_row, snap_row)
                pr.acq("pe", BI[bb[0]]); pr.acq("pe", BI[bb[1]])
                em("pe", "matmul", psBC[:, 0:512], onr[:, :], RW[1], start=True, stop=True)
                em("pe", "matmul", psBC[:, 512:1024], onr[:, :], RW[2], start=True, stop=True,
                   inc=sm.s_bcm)
                snap_bc = sm.s_bcm.n
                for dc in range(NDC):
                    if dc == 0:
                        wt("dve", sm.s_bcm, snap_bc)
                    em("dve", "tensor_tensor", tmp2[:, :],
                       xn_sl(dc, t0), psBC[:, 0:512], ALU.mult)
                    em("dve", "tensor_tensor", xn_sl(dc, t0), tmp2[:, :],
                       psBC[:, 512:1024], ALU.add, inc=sm.s_xn)
                snap_xn[tb] = sm.s_xn.n
                pr.rel(BI[bb[0]], sm.s_xn); pr.rel(BI[bb[1]], sm.s_xn)

            a1_pass1(0)
            for tb in range(1, NTB):
                a1_pass1(tb)
                a1_pass2(tb - 1)
            a1_pass2(NTB - 1)

            wt("pe", sm.s_msk, snap_msv)
            wt("pe", sm.s_wv, snap_wv)
            for tc in range(NTC):
                tb = tc // 4
                bank = ["C0", "C1"] if tc % 2 == 0 else ["D0", "D1"]
                psv = psC if tc % 2 == 0 else psD
                wt("pe", sm.s_xn, snap_xn[tb])
                pr.acq("pe", BI[bank[0]]); pr.acq("pe", BI[bank[1]])
                for dc in range(NDC):
                    lhs = A_xnh[:, dc * 2048 + tc * 128: dc * 2048 + (tc + 1) * 128]
                    em("pe", "matmul", psv[:, 0:512], lhs,
                       A_wsm[:, dc * 1024: dc * 1024 + 512],
                       start=(dc == 0), stop=(dc == 7), skip_group_check=True)
                    em("pe", "matmul", psv[:, 512:1024], lhs,
                       A_wsm[:, dc * 1024 + 512: (dc + 1) * 1024],
                       start=(dc == 0), stop=(dc == 7), skip_group_check=True,
                       inc=(sm.s_vmm if dc == 7 else None))
                svm = sm.s_vmm.n
                wt("dve", sm.s_vmm, svm)
                if tc == 0:
                    wt("dve", sm.s_msk, snap_msv)
                for half in range(2):
                    dst = A_vact[:, tc * 1040 + half * 520: tc * 1040 + half * 520 + 520]
                    dst = dst.rearrange("p (n x) -> p n x", n=8)[:, :, 0:64]
                    src = psv[:, half * 512:(half + 1) * 512].rearrange("p (n h) -> p n h", n=8)
                    em("dve", "tensor_copy", dst, src,
                       inc=(sm.s_vev if half == 1 else None))
                vev_hist[tc] = sm.s_vev.n
                pr.rel(BI[bank[0]], sm.s_vev); pr.rel(BI[bank[1]], sm.s_vev)
            snap_vall = sm.s_vev.n

            for hp in range(NH // 2):
                wqk_r.acq("sync", 0)
                em("sync", "dma_start",
                   out=wqkh[:, 0:2048].rearrange("p (c v) -> p c v", c=NDC),
                   in_=wqk_s[hp].rearrange("c p v -> p c v"), inc=sm.s_wqk, amt=16)
                snap_wqk = sm.s_wqk.n
                wt("pe", sm.s_wqk, snap_wqk)
                for tb in range(NTB):
                    wt("pe", sm.s_xn, snap_xn[tb])
                    for bank, dst, coff in (("D0", q_sb, 0), ("D1", k_sb, 128)):
                        pr.acq("pe", BI[bank])
                        for dc in range(NDC):
                            em("pe", "matmul", PS[bank],
                               wqkh[:, dc * 256 + coff: dc * 256 + coff + 128],
                               A_xnh[:, dc * 2048 + tb * 512: dc * 2048 + (tb + 1) * 512],
                               start=(dc == 0), stop=(dc == 7), skip_group_check=True,
                               inc=(sm.s_qkmm if dc == 7 else None))
                        sqm = sm.s_qkmm.n
                        wt("dve", sm.s_qkmm, sqm)
                        if hp > 0 and tb == 0:
                            wt("dve", sm.s_scm, sm.s_scm.n)
                        em("dve", "tensor_copy", dst[:, tb * 512:(tb + 1) * 512],
                           PS[bank], inc=sm.s_qkev)
                        pr.rel(BI[bank], sm.s_qkev)
                snap_qkev = sm.s_qkev.n
                wqk_r.rel(0, sm.s_qkmm)
                wt("pe", sm.s_qkev, snap_qkev)
                for sub in range(2):
                    n = 2 * hp + sub
                    pbase = 64 * sub
                    for Q in range(4):
                        jmax = 4 * Q + 4
                        pvbank = "B0" if Q % 2 == 0 else "B1"
                        pvb = psB[:, (Q % 2) * 512:(Q % 2) * 512 + 260]
                        pr.acq("pe", BI[pvbank])
                        for j in range(jmax):
                            diag = j >= 4 * Q
                            sb = jg % 6
                            sbank = ("A0", "A1", "C0", "C1", "D0", "D1")[sb]
                            if jg >= 6:
                                wt("pe", sm.s_exp, exp_hist[jg - 6])
                            pr.acq("pe", BI[sbank])
                            em("pe", "matmul", PS[sbank],
                               k_sb[pbase:pbase + 64, j * 128:(j + 1) * 128],
                               q_sb[pbase:pbase + 64, Q * 512:(Q + 1) * 512],
                               start=True, stop=True, skip_group_check=True, inc=sm.s_scm)
                            sscm = sm.s_scm.n
                            wt("act", sm.s_scm, sscm)
                            if jg >= 6:
                                wt("act", sm.s_pvmm, pvmm_hist[jg - 6])
                            E_r.acq("act", sb)
                            em("act", "activation", E_sb[:, sb * 512:(sb + 1) * 512],
                               PS[sbank], AF.Exp, scale=0.125, inc=sm.s_exp,
                               chain=False)
                            exp_hist[jg] = sm.s_exp.n
                            pr.rel(BI[sbank], sm.s_exp)
                            if diag:
                                c = j - 4 * Q
                                off = 384 - 128 * c
                                wt("dve", sm.s_exp, exp_hist[jg])
                                em("dve", "tensor_tensor", E_sb[:, sb * 512:(sb + 1) * 512],
                                   E_sb[:, sb * 512:(sb + 1) * 512],
                                   mask_sb[:, off:off + 512], ALU.mult, inc=sm.s_msk,
                                   chain=False)
                                wt("pe", sm.s_msk, sm.s_msk.n)
                            else:
                                wt("pe", sm.s_exp, exp_hist[jg])
                            for qt in range(4):
                                em("pe", "matmul", pvb[:, qt * 65:qt * 65 + 65],
                                   E_sb[:, sb * 512 + qt * 128: sb * 512 + (qt + 1) * 128],
                                   A_vact[:, j * 1040 + n * 65: j * 1040 + (n + 1) * 65],
                                   start=(j == 0 and qt == 0),
                                   stop=(j == jmax - 1 and qt == 3),
                                   skip_group_check=True, inc=sm.s_pvmm)
                            pvmm_hist[jg] = sm.s_pvmm.n
                            E_r.rel(sb, sm.s_pvmm)
                            jg += 1
                        spv = sm.s_pvmm.n
                        wt("dve", sm.s_pvmm, spv)
                        oQ = (Q % 2) * 256
                        out_r.acq("dve", Q % 2)
                        zv = pvb.rearrange("p (q x) -> p q x", q=4)[:, :, 64:65]
                        em("dve", "reciprocal",
                           rz_sb[:, (Q % 2) * 4:(Q % 2) * 4 + 4].rearrange(
                               "p (q x) -> p q x", q=4), zv)
                        for qt in range(4):
                            em("dve", "tensor_scalar",
                               outsb[:, oQ + qt * 64: oQ + (qt + 1) * 64],
                               pvb[:, qt * 65:qt * 65 + 64],
                               rz_sb[:, (Q % 2) * 4 + qt:(Q % 2) * 4 + qt + 1], None,
                               ALU.mult, inc=sm.s_pvev)
                        pvev_hist[Qg] = sm.s_pvev.n
                        pr.rel(BI[pvbank], sm.s_pvev)
                        wt("pool", sm.s_pvev, sm.s_pvev.n)
                        em("pool", "dma_start",
                           out=scr[n, Q * 512:(Q + 1) * 512, :].rearrange(
                               "(q p) h -> p q h", p=128),
                           in_=outsb[:, oQ:oQ + 256].rearrange("p (q h) -> p q h", q=4),
                           inc=s_scr[Q % 2], amt=16)
                        out_r.rel(Q % 2, s_scr[Q % 2])
                        Qg += 1
            snap_exp_end = sm.s_exp.n
            w13_r.g[0].append((sm.s_qkmm, sm.s_qkmm.n))
            w13_r.g[1].append((sm.s_qkmm, sm.s_qkmm.n))

            wt("pool", s_scr[0], s_scr[0].n)
            wt("pool", s_scr[1], s_scr[1].n)
            for dc in range(NDC):
                for tb in range(NTB):
                    i = dc * NTB + tb
                    sl4 = i % 2
                    sl2 = i % 2
                    xat_r.acq("pool", sl4)
                    em("pool", "dma_start", out=xattn[:, sl4 * 512:(sl4 + 1) * 512],
                       in_=scr[2 * dc:2 * dc + 2].rearrange(
                           "n (q r) h -> (n q) (r h)", q=64)[:, tb * 512:(tb + 1) * 512],
                       inc=s_rb[sl4], amt=16)
                    srb = s_rb[sl4].n
                    xr.acq("sync", sl2)
                    em("sync", "dma_start", out=xring[:, 512 * sl2:512 * (sl2 + 1)],
                       in_=x_fm[b, dc * 128:(dc + 1) * 128, tb * 512:(tb + 1) * 512],
                       inc=s_xld[sl2], amt=16)
                    sxl = s_xld[sl2].n
                    wt("dve", s_rb[sl4], srb)
                    wt("dve", s_xld[sl2], sxl)
                    em("dve", "scalar_tensor_tensor", x2_sl(dc, tb * 512),
                       xattn[:, sl4 * 512:(sl4 + 1) * 512], wsum_c[:, dc:dc + 1],
                       xring[:, 512 * sl2:512 * (sl2 + 1)], ALU.mult, ALU.add,
                       inc=sm.s_x2)
                    xr.rel(sl2, sm.s_x2)
                    xat_r.rel(sl4, sm.s_x2)
            x2_all = sm.s_x2.n

            w2_r.g[0] = [(sm.s_vmm, sm.s_vmm.n)]
            w2_r.g[1] = [(sm.s_vmm, sm.s_vmm.n)]

            for tb in range(NTB):
                t0 = tb * TBS
                for dc in range(NDC):
                    i = tb * NDC + dc
                    sl = i % 2
                    wt("act", sm.s_x2, x2_all)
                    E_r.acq("act", sl)
                    em("act", "activation", xsqr[:, 512 * sl:512 * (sl + 1)],
                       x2_sl(dc, t0), AF.Square, inc=sm.s_sq)
                    ssq = sm.s_sq.n
                    wt("dve", sm.s_x2, x2_all)
                    sil_r.acq("dve", sl)
                    em("dve", "tensor_copy", xbfr[:, 512 * sl:512 * (sl + 1)],
                       x2_sl(dc, t0), inc=sm.s_xbf)
                    sbf = sm.s_xbf.n
                    wt("pe", sm.s_xbf, sbf)
                    wt("pe", sm.s_sq, ssq)
                    if dc == 0:
                        pr.acq("pe", BI["D0"]); pr.acq("pe", BI["D1"])
                    em("pe", "matmul", psD[0:1, 0:512], oncb[:, :],
                       xbfr[:, 512 * sl:512 * (sl + 1)],
                       start=(dc == 0), stop=(dc == 7), skip_group_check=True)
                    em("pe", "matmul", psD[0:1, 512:1024], oncb[:, :],
                       xsqr[:, 512 * sl:512 * (sl + 1)],
                       start=(dc == 0), stop=(dc == 7), skip_group_check=True,
                       inc=sm.s_stat)
                    E_r.rel(sl, sm.s_stat)
                    sil_r.rel(sl, sm.s_stat)
                snap_stat = sm.s_stat.n
                wt("pe", sm.s_x2, x2_all)
                pr.acq("pe", BI["A0"])
                for dc in range(NDC):
                    em("pe", "matmul", psA[0:8, 0:512], gwp[:, 8 * dc:8 * dc + 8],
                       x2_sl(dc, t0), start=(dc == 0), stop=(dc == 7),
                       inc=(sm.s_gmm if dc == 7 else None))
                snap_gmm = sm.s_gmm.n
                wt("dve", sm.s_gmm, snap_gmm)
                em("dve", "tensor_copy", gl8, psA[0:8, 0:512], inc=sm.s_f1)
                pr.rel(BI["A0"], sm.s_f1)
                wt("dve", sm.s_stat, snap_stat)
                wt("dve", sm.s_bcm, sm.s_bcm.n)
                wt("dve", sm.s_pe, sm.s_pe.n)
                em("dve", "tensor_scalar", RW[0], psD[0:1, 0:512], 1.0 / 1024, None, ALU.mult)
                em("dve", "tensor_scalar", RW[1], psD[0:1, 512:1024], 1.0 / 1024, None, ALU.mult)
                em("dve", "tensor_tensor", RW[2], RW[0], RW[0], ALU.mult)
                em("dve", "tensor_tensor", RW[1], RW[1], RW[2], ALU.subtract, inc=sm.s_row)
                pr.rel(BI["D0"], sm.s_row); pr.rel(BI["D1"], sm.s_row)
                wt("act", sm.s_row, sm.s_row.n)
                em("act", "activation", RW[2], RW[1], AF.Sqrt, bias=1e-5, inc=sm.s_sil)
                wt("dve", sm.s_sil, sm.s_sil.n)
                em("dve", "reciprocal", RW[2], RW[2])
                em("dve", "tensor_tensor", RW[3], RW[1], RW[2], ALU.mult)
                em("dve", "tensor_tensor", RW[3], RW[3], RW[2], ALU.mult, inc=sm.s_row)
                wt("act", sm.s_row, sm.s_row.n)
                em("act", "activation", RW[1], RW[3], AF.Sqrt, bias=1e-6, inc=sm.s_sil)
                wt("dve", sm.s_sil, sm.s_sil.n)
                em("dve", "reciprocal", RW[1], RW[1])
                em("dve", "tensor_tensor", RW[3], RW[2], RW[1], ALU.mult)
                em("dve", "tensor_tensor", RW[0], RW[0], RW[2], ALU.mult)
                em("dve", "tensor_scalar", RW[0], RW[0], -1.0, None, ALU.mult)
                em("dve", "tensor_tensor", RW[1], RW[0], RW[1], ALU.mult, inc=sm.s_row)
                snap_row = sm.s_row.n
                wt("pe", sm.s_row, snap_row)
                for bk in ("B0", "B1", "C0", "C1"):
                    pr.acq("pe", BI[bk])
                em("pe", "matmul", psB[:, 0:512], onr[:, :], RW[2], start=True, stop=True)
                em("pe", "matmul", psB[:, 512:1024], onr[:, :], RW[0], start=True, stop=True)
                em("pe", "matmul", psC[:, 0:512], onr[:, :], RW[3], start=True, stop=True)
                em("pe", "matmul", psC[:, 512:1024], onr[:, :], RW[1], start=True, stop=True,
                   inc=sm.s_bcm)
                wt("act", sm.s_bcm, sm.s_bcm.n)
                em("act", "activation", psB[:, 0:512], psB[:, 0:512], AF.Copy, bias=1.0,
                   inc=sm.s_sil)
                snap_r1 = sm.s_sil.n
                wt("pe", sm.s_f1, sm.s_f1.n)
                pr.acq("pe", BI["D1"])
                for qt in range(4):
                    em("pe", "transpose", psD[:, 512 + qt * 8:520 + qt * 8],
                       gl8f[0:8, qt * 128:(qt + 1) * 128], idn_sb[0:8, 0:8])
                for qt in range(4):
                    em("pe", "transpose", psD[:, 544 + qt:545 + qt],
                       RW[3][0:1, qt * 128:(qt + 1) * 128], idn_sb[0:1, 0:1])
                for qt in range(4):
                    em("pe", "transpose", psD[:, 548 + qt:549 + qt],
                       RW[1][0:1, qt * 128:(qt + 1) * 128], idn_sb[0:1, 0:1],
                       inc=(sm.s_pe if qt == 3 else None))
                snap_tr = sm.s_pe.n
                BCS = (128, 4, 8)
                def c3(a_, b_):
                    return rtv[:, a_:b_].rearrange("p (q e) -> p q e", q=4)
                def c1(a_):
                    return rtv[:, a_:a_ + 4].rearrange("p (q x) -> p q x", q=4)
                wt("dve", sm.s_pe, snap_tr)
                em("dve", "tensor_copy", rtv[:, 0:40], psD[:, 512:552])
                pr.rel(BI["D1"], selfsem["dve"])
                sbc3 = sbc[:, :].rearrange("p (x e) -> p x e", x=1).broadcast_to(BCS)
                em("dve", "tensor_tensor", c3(40, 72), sbc3,
                   c1(36).broadcast_to(BCS), ALU.mult)
                em("dve", "tensor_tensor", c3(72, 104), c3(0, 32),
                   c1(32).broadcast_to(BCS), ALU.mult)
                em("dve", "tensor_tensor", c3(72, 104), c3(72, 104), c3(40, 72), ALU.add)
                em("dve", "tensor_reduce", c1(40), c3(72, 104), axis=AX, op=ALU.max)
                em("dve", "tensor_tensor", c3(72, 104), c3(72, 104),
                   c1(40).broadcast_to(BCS), ALU.subtract, inc=sm.s_f1)
                wt("act", sm.s_f1, sm.s_f1.n)
                em("act", "activation", rtv[:, 104:136], rtv[:, 72:104], AF.Exp,
                   inc=sm.s_sil)
                wt("dve", sm.s_sil, sm.s_sil.n)
                em("dve", "tensor_reduce", c1(44), c3(104, 136), axis=AX, op=ALU.add)
                em("dve", "reciprocal", rtv[:, 48:52], rtv[:, 44:48])
                em("dve", "tensor_tensor", c3(136, 168), c3(104, 136),
                   c1(48).broadcast_to(BCS), ALU.mult)
                em("dve", "tensor_reduce", c1(52), c3(136, 168), axis=AX, op=ALU.max)
                em("dve", "tensor_tensor", c3(168, 200), c3(136, 168),
                   c1(52).broadcast_to(BCS), ALU.is_equal)
                em("dve", "scalar_tensor_tensor", rtv[:, 200:232], rtv[:, 168:200],
                   -2.0, rtv[:, 136:168], ALU.mult, ALU.add)
                em("dve", "tensor_reduce", c1(56), c3(200, 232), axis=AX, op=ALU.max)
                em("dve", "tensor_tensor", rtv[:, 60:64], rtv[:, 52:56],
                   rtv[:, 56:60], ALU.add)
                em("dve", "reciprocal", rtv[:, 64:68], rtv[:, 60:64])
                em("dve", "tensor_tensor", rtv[:, 68:72], rtv[:, 52:56],
                   rtv[:, 64:68], ALU.mult)
                em("dve", "tensor_tensor", rtv[:, 232:236], rtv[:, 56:60],
                   rtv[:, 64:68], ALU.mult)
                em("dve", "tensor_tensor", c3(104, 136), c3(200, 232),
                   c1(56).broadcast_to(BCS), ALU.is_equal)
                em("dve", "tensor_tensor", c3(72, 104), c3(168, 200),
                   c1(68).broadcast_to(BCS), ALU.mult)
                em("dve", "tensor_tensor", c3(136, 168), c3(104, 136),
                   c1(232).broadcast_to(BCS), ALU.mult)
                em("dve", "tensor_tensor", cw_all[:, :], rtv[:, 72:104],
                   rtv[:, 136:168], ALU.add, inc=sm.s_cwt)
                snap_cwt = sm.s_cwt.n
                wt("dve", sm.s_sil, snap_r1)
                wt("dve", sm.s_gmm, snap_gmm)
                for dc in range(NDC):
                    em("dve", "tensor_tensor", tmp2[:, :], x2_sl(dc, t0),
                       psC[:, 0:512], ALU.mult)
                    em("dve", "tensor_tensor", tmp2[:, :], tmp2[:, :],
                       psC[:, 512:1024], ALU.add)
                    em("dve", "tensor_scalar", xn_sl(dc, t0), tmp2[:, :],
                       rms_c[:, dc:dc + 1], None, ALU.mult, inc=sm.s_hrdy)
                    em("dve", "tensor_tensor", tmp2[:, :], x2_sl(dc, t0),
                       psB[:, 0:512], ALU.mult)
                    em("dve", "tensor_tensor", x2_sl(dc, t0), tmp2[:, :],
                       psB[:, 512:1024], ALU.add, inc=sm.s_acc)
                snap_h = sm.s_hrdy.n
                snap_acc = sm.s_acc.n
                pr.rel(BI["B0"], sm.s_acc); pr.rel(BI["B1"], sm.s_acc)
                pr.rel(BI["C0"], sm.s_hrdy); pr.rel(BI["C1"], sm.s_hrdy)


                def build_cw_row(et):
                    sl = et % 2
                    wt("pe", sm.s_cwt, snap_cwt)
                    if et >= 2:
                        wt("pe", sm.s_bcw, cw_slot_user[sl])
                    pr.acq("pe", BI["D1"])
                    for qt in range(4):
                        em("pe", "transpose", psD[0:1, 512 + qt * 128:640 + qt * 128],
                           cw_all[:, qt * 8 + et: qt * 8 + et + 1], idn_sb[:, :],
                           inc=(sm.s_bcw if qt == 3 else None))
                    tsnap = sm.s_bcw.n
                    wt("dve", sm.s_bcw, tsnap)
                    for qt in range(4):
                        em("dve", "tensor_copy", cwT_e[0:1, sl * 512 + qt * 128:
                                                       sl * 512 + (qt + 1) * 128],
                           psD[0:1, 512 + qt * 128:640 + qt * 128],
                           inc=(sm.s_cwt if qt == 3 else None))
                    pr.rel(BI["D1"], sm.s_cwt)
                    cw_built[et] = sm.s_cwt.n

                cw_built = {}
                cw_slot_user = {0: 0, 1: 0}
                build_cw_row(0)
                for e in range(NE):
                    bcbank = "D0" if e % 2 == 0 else "D1"
                    wt("pe", sm.s_cwt, cw_built[e])
                    pr.acq("pe", BI[bcbank])
                    em("pe", "matmul", PS[bcbank], onr[:, :],
                       cwT_e[0:1, (e % 2) * 512:(e % 2) * 512 + 512],
                       start=True, stop=True, inc=sm.s_bcw)
                    cw_slot_user[e % 2] = sm.s_bcw.n
                    wt("dve", sm.s_bcw, sm.s_bcw.n)
                    em("dve", "tensor_copy", bc_sb[:, :], PS[bcbank], inc=sm.s_bcc)
                    pr.rel(BI[bcbank], sm.s_bcc)
                    if e + 1 < NE:
                        build_cw_row(e + 1)

                    for ib in range(NIBLK):
                        k = (s_w13[0].n + s_w13[1].n) // 32
                        wb = k % 2
                        w13_r.acq("sync", wb)
                        em("sync", "dma_start",
                           out=A_w13[:, wb * 2048:(wb + 1) * 2048].rearrange(
                               "p (c v) -> p c v", c=NDC),
                           in_=w1_s[e, ib].rearrange("c p i -> p c i"),
                           inc=s_w13[wb], amt=16)
                        em("sync", "dma_start",
                           out=A_w13[:, 4096 + wb * 2048:4096 + (wb + 1) * 2048].rearrange(
                               "p (c v) -> p c v", c=NDC),
                           in_=w3_s[e, ib].rearrange("c p i -> p c i"),
                           inc=s_w13[wb], amt=16)
                        sw13 = s_w13[wb].n
                        wt("pe", s_w13[wb], sw13)
                        if ib == 0:
                            wt("pe", sm.s_hrdy, snap_h)
                            if e == 0:
                                wt("pe", sm.s_acc, snap_acc)
                                wt("pe", sm.s_exp, snap_exp_end)
                            else:
                                wt("pe", sm.s_peo, sm.s_peo.n)
                        for ic in range(2):
                            gidx = sm.s_pg.n
                            pb = gidx % 2
                            abank = "A0" if pb == 0 else "A1"
                            gbank = "B0" if pb == 0 else "B1"
                            pr.acq("pe", BI[abank]); pr.acq("pe", BI[gbank])
                            for dc in range(NDC):
                                em("pe", "matmul", PS[abank],
                                   A_w13[:, wb * 2048 + dc * 256 + ic * 128:
                                         wb * 2048 + dc * 256 + (ic + 1) * 128],
                                   xn_sl(dc, t0),
                                   start=(dc == 0), stop=(dc == 7),
                                   skip_group_check=True,
                                   inc=(sm.s_pa if dc == 7 else None))
                            spa = sm.s_pa.n
                            for dc in range(NDC):
                                em("pe", "matmul", PS[gbank],
                                   A_w13[:, 4096 + wb * 2048 + dc * 256 + ic * 128:
                                         4096 + wb * 2048 + dc * 256 + (ic + 1) * 128],
                                   xn_sl(dc, t0),
                                   start=(dc == 0), stop=(dc == 7),
                                   skip_group_check=True,
                                   inc=(sm.s_pg if dc == 7 else None))
                            spg = sm.s_pg.n
                            wt("act", sm.s_pa, spa)
                            sil_r.acq("act", gidx % 2)
                            em("act", "activation",
                               sil_sb[:, (gidx % 2) * 512:(gidx % 2 + 1) * 512],
                               PS[abank], AF.Silu, inc=sm.s_sil)
                            gi_sil[gidx] = sm.s_sil.n
                            pr.rel(BI[abank], sm.s_sil)
                            gl = gidx % 28
                            wt("dve", sm.s_sil, gi_sil[gidx])
                            wt("dve", sm.s_pg, spg)
                            em("dve", "tensor_tensor",
                               A_vact[:, gl * 512:(gl + 1) * 512],
                               sil_sb[:, (gidx % 2) * 512:(gidx % 2 + 1) * 512],
                               PS[gbank], ALU.mult, inc=sm.s_mul)
                            gi_mul[gidx] = sm.s_mul.n
                            sil_r.rel(gidx % 2, sm.s_mul)
                            pr.rel(BI[gbank], sm.s_mul)
                        w13_r.rel(wb, sm.s_pg)
                    snap_mul_all = sm.s_mul.n
                    wt("pe", sm.s_mul, snap_mul_all)
                    for dc in range(NDC):
                        slw2 = (s_w2[0].n + s_w2[1].n) // 16
                        wb2 = slw2 % 2
                        w2_r.acq("pool", wb2)
                        em("pool", "dma_start",
                           out=A_wsm[:, wb2 * 3584:(wb2 + 1) * 3584].rearrange(
                               "p (c v) -> p c v", c=NIC),
                           in_=w2_s[e, dc].rearrange("c p d -> p c d"),
                           inc=s_w2[wb2], amt=16)
                        sw2 = s_w2[wb2].n
                        wt("pe", s_w2[wb2], sw2)
                        pbo = sm.s_peo.n % 2
                        obank = "C0" if pbo == 0 else "C1"
                        pr.acq("pe", BI[obank])
                        for ic2 in range(NIC):
                            em("pe", "matmul", PS[obank],
                               A_wsm[:, wb2 * 3584 + ic2 * 128:wb2 * 3584 + (ic2 + 1) * 128],
                               A_vact[:, ic2 * 512:(ic2 + 1) * 512],
                               start=(ic2 == 0), stop=(ic2 == NIC - 1),
                               skip_group_check=True,
                               inc=(sm.s_peo if ic2 == NIC - 1 else None))
                        speo = sm.s_peo.n
                        w2_r.rel(wb2, sm.s_peo)
                        wt("dve", sm.s_peo, speo)
                        em("dve", "tensor_tensor", tmp2[:, :], PS[obank],
                           bc_sb[:, :], ALU.mult)
                        em("dve", "tensor_tensor", x2_sl(dc, t0), x2_sl(dc, t0),
                           tmp2[:, :], ALU.add, inc=sm.s_drn)
                        pr.rel(BI[obank], sm.s_drn)
                snap_tb_done = sm.s_drn.n
                wt("sync", sm.s_drn, snap_tb_done)
                for dc in range(NDC):
                    em("sync", "dma_start",
                       out=out_fm[b, dc * 128:(dc + 1) * 128, t0:t0 + TBS],
                       in_=x2_sl(dc, t0), inc=sm.s_out, amt=16)
            wt("sync", sm.s_out, sm.s_out.n)
            barrier()

        wt("sync", sm.s_out, sm.s_out.n)

        @block.tensor
        def _(eng):
            for f in ops["pe"]:
                f(eng)

        @block.scalar
        def _(eng):
            for f in ops["act"]:
                f(eng)

        @block.vector
        def _(eng):
            for f in ops["dve"]:
                f(eng)

        @block.sync
        def _(eng):
            for f in ops["sync"]:
                f(eng)

        @block.gpsimd
        def _(eng):
            for f in ops["pool"]:
                f(eng)

    return nc


def kernel(L0_attn_x, L0_attn_W_qkv, L0_attn_W_out, L0_ffn_rms_weight,
           L0_ffn_gate_w, L0_ffn_w1, L0_ffn_w2, L0_ffn_w3):
    x = np.asarray(L0_attn_x, dtype=np.float32)
    W_qkv = np.asarray(L0_attn_W_qkv, dtype=np.float32)
    W_out = np.asarray(L0_attn_W_out, dtype=np.float32)
    rms = np.asarray(L0_ffn_rms_weight, dtype=np.float32)
    gw = np.asarray(L0_ffn_gate_w, dtype=np.float32)
    w1 = np.asarray(L0_ffn_w1, dtype=np.float32)
    w2 = np.asarray(L0_ffn_w2, dtype=np.float32)
    w3 = np.asarray(L0_ffn_w3, dtype=np.float32)

    qh = W_qkv[:, :, 0:64]
    kh = W_qkv[:, :, 64:128]
    qkp = np.stack([qh.reshape(8, 2, D, 64), kh.reshape(8, 2, D, 64)], axis=1)
    wqk_tl = np.ascontiguousarray(
        qkp.transpose(0, 3, 1, 2, 4).reshape(NH // 2, NDC, 128, 256))
    wv_tl = np.ascontiguousarray(
        W_qkv[:, :, 128:192].transpose(1, 0, 2).reshape(NDC, 128, NH * HD))
    wout2d = np.ascontiguousarray(W_out.reshape(D, D))
    rms_tl = np.ascontiguousarray(rms.reshape(NDC, 128).T)
    gw_tl = np.ascontiguousarray(gw.T.reshape(NDC, 128, NE))
    w1_tl = np.ascontiguousarray(
        w1.transpose(0, 2, 1).reshape(NE, NDC, 128, NIBLK, 256).transpose(0, 3, 1, 2, 4))
    w3_tl = np.ascontiguousarray(
        w3.transpose(0, 2, 1).reshape(NE, NDC, 128, NIBLK, 256).transpose(0, 3, 1, 2, 4))
    w2_tl = np.ascontiguousarray(
        w2.transpose(0, 2, 1).reshape(NE, NIC, 128, NDC, 128).transpose(0, 3, 1, 2, 4))
    idn = np.eye(128, dtype=np.float32)
    onc_np = np.ones((128, 1), dtype=np.float32)
    oncb_np = np.ones((128, 1), dtype=ml_dtypes.bfloat16)
    onr_np = np.ones((1, 128), dtype=np.float32)
    mask_np = (np.arange(896)[None, :] - np.arange(128)[:, None] >= 384).astype(
        ml_dtypes.bfloat16)
    eps_np = np.tile(np.array([[1e-5, 1e-6]], dtype=np.float32), (128, 1))

    nc = build_nc()
    in_maps = []
    for c in range(8):
        xs = np.ascontiguousarray(x[4 * c:4 * c + 4].transpose(0, 2, 1))
        in_maps.append({
            "x_fm": xs, "wqk_t": wqk_tl, "wv_t": wv_tl, "wout2d": wout2d,
            "rms_t": rms_tl, "gw_t": gw_tl, "w1_t": w1_tl, "w3_t": w3_tl,
            "w2_t": w2_tl, "idn": idn, "onc": onc_np, "oncb": oncb_np,
            "onr": onr_np, "mask": mask_np, "epsc": eps_np,
        })
    res = run_bass_kernel_spmd(nc, in_maps, core_ids=list(range(8)))
    outs = [np.ascontiguousarray(r["out_fm"].transpose(0, 2, 1)) for r in res.results]
    return np.concatenate(outs, axis=0).astype(np.float32)


# version 05 (final)
# speedup vs baseline: 1.1015x; 1.1015x over previous
import numpy as np
import ml_dtypes
from contextlib import ExitStack

import concourse.bass as bass
import concourse.mybir as mybir
from concourse.bass_utils import run_bass_kernel_spmd

F32 = mybir.dt.float32
BF16 = mybir.dt.bfloat16
AF = mybir.ActivationFunctionType
ALU = mybir.AluOpType
AX = mybir.AxisListType.X

NB = 4
S = 2048
D = 1024
NDC = 8
NH = 16
HD = 64
NTB = 4
TBS = 512
NTC = 16
NE = 8
NI = 3584
NIBLK = 14
NIC = 28


def build_nc():
    nc = bass.Bass("TRN2", target_bir_lowering=False)

    x_fm = nc.dram_tensor("x_fm", [NB, D, S], F32, kind="ExternalInput").ap()
    wqk_t = nc.dram_tensor("wqk_t", [NH // 2, NDC, 128, 256], F32, kind="ExternalInput").ap()
    wv_t = nc.dram_tensor("wv_t", [NDC, 128, NH * HD], F32, kind="ExternalInput").ap()
    wout2d = nc.dram_tensor("wout2d", [D, D], F32, kind="ExternalInput").ap()
    rms_t = nc.dram_tensor("rms_t", [128, NDC], F32, kind="ExternalInput").ap()
    gw_t = nc.dram_tensor("gw_t", [NDC, 128, NE], F32, kind="ExternalInput").ap()
    w1_t = nc.dram_tensor("w1_t", [NE, NIBLK, NDC, 128, 256], F32, kind="ExternalInput").ap()
    w3_t = nc.dram_tensor("w3_t", [NE, NIBLK, NDC, 128, 256], F32, kind="ExternalInput").ap()
    w2_t = nc.dram_tensor("w2_t", [NE, NDC, NIC, 128, 128], F32, kind="ExternalInput").ap()
    idn_in = nc.dram_tensor("idn", [128, 128], F32, kind="ExternalInput").ap()
    onc_in = nc.dram_tensor("onc", [128, 1], F32, kind="ExternalInput").ap()
    oncb_in = nc.dram_tensor("oncb", [128, 1], BF16, kind="ExternalInput").ap()
    onr_in = nc.dram_tensor("onr", [1, 128], F32, kind="ExternalInput").ap()
    mask_in = nc.dram_tensor("mask", [128, 896], BF16, kind="ExternalInput").ap()
    eps_in = nc.dram_tensor("epsc", [128, 2], F32, kind="ExternalInput").ap()
    out_fm = nc.dram_tensor("out_fm", [NB, D, S], F32, kind="ExternalOutput").ap()

    w1_s = nc.dram_tensor("w1_s", [NE, NIBLK, NDC, 128, 256], BF16, kind="Internal").ap()
    w3_s = nc.dram_tensor("w3_s", [NE, NIBLK, NDC, 128, 256], BF16, kind="Internal").ap()
    w2_s = nc.dram_tensor("w2_s", [NE, NDC, NIC, 128, 128], BF16, kind="Internal").ap()
    wqk_s = nc.dram_tensor("wqk_s", [NH // 2, NDC, 128, 256], BF16, kind="Internal").ap()
    wv_s = nc.dram_tensor("wv_s", [NDC, 128, NH * HD], BF16, kind="Internal").ap()
    scr = nc.dram_tensor("scr", [NH, S, HD], BF16, kind="Internal").ap()

    with ExitStack() as stack:
        e_ = stack.enter_context

        A_xnh = e_(nc.sbuf_tensor("A_xnh", [128, 16384], BF16))
        A_vact = e_(nc.sbuf_tensor("A_vact", [128, 16640], BF16))
        A_w13 = e_(nc.sbuf_tensor("A_w13", [128, 8192], BF16))
        A_wsm = e_(nc.sbuf_tensor("A_wsm", [128, 8192], BF16))
        wqkh = A_w13
        q_sb = e_(nc.sbuf_tensor("q_sb", [128, 2048], BF16))
        k_sb = e_(nc.sbuf_tensor("k_sb", [128, 2048], BF16))
        E_sb = e_(nc.sbuf_tensor("E_sb", [128, 3072], BF16))
        sil_sb = e_(nc.sbuf_tensor("sil_sb", [128, 1024], BF16))
        xattn = e_(nc.sbuf_tensor("xattn", [128, 1024], BF16))
        outsb = e_(nc.sbuf_tensor("outsb", [128, 512], BF16))
        mask_sb = e_(nc.sbuf_tensor("mask_sb", [128, 896], BF16))
        oncb = e_(nc.sbuf_tensor("oncb_sb", [128, 1], BF16))
        xsqr = E_sb
        xbfr = sil_sb

        x2acc = e_(nc.sbuf_tensor("x2acc", [128, 16384], F32))
        xring = e_(nc.sbuf_tensor("xring", [128, 1024], F32))
        tmp2 = e_(nc.sbuf_tensor("tmp2", [128, 512], F32))
        bc_sb = e_(nc.sbuf_tensor("bc_sb", [128, 512], F32))
        rows = e_(nc.sbuf_tensor("rows", [1, 2048], F32))

        cw_all = e_(nc.sbuf_tensor("cw_all", [128, 32], F32))
        gl8f = e_(nc.sbuf_tensor("gl8f", [8, 1024], F32))
        gl8 = gl8f[:, 0:512]
        cwT_e = gl8f[0:1, 0:1024]
        s_sb = e_(nc.sbuf_tensor("s_sb", [8, 1], F32))
        srow = e_(nc.sbuf_tensor("srow", [1, 8], F32))
        rtv = e_(nc.sbuf_tensor("rtv", [128, 240], F32))
        rz_sb = e_(nc.sbuf_tensor("rz_sb", [128, 8], F32))
        wsum_c = e_(nc.sbuf_tensor("wsum_c", [128, 8], F32))
        rms_c = e_(nc.sbuf_tensor("rms_c", [128, 8], F32))
        gw_sb = e_(nc.sbuf_tensor("gw_sb", [128, 64], F32))
        gwp = e_(nc.sbuf_tensor("gwp", [128, 64], F32))
        sbc = e_(nc.sbuf_tensor("sbc", [128, 8], F32))
        idn_sb = e_(nc.sbuf_tensor("idn_sb", [128, 128], F32))
        onc = e_(nc.sbuf_tensor("onc_sb", [128, 1], F32))
        onr = e_(nc.sbuf_tensor("onr_sb", [1, 128], F32))
        epsc = e_(nc.sbuf_tensor("epsc_sb", [128, 2], F32))
        nc.const_aps.register(F32, 1e-5, epsc[:, 0:1])
        nc.const_aps.register(F32, 1e-6, epsc[:, 1:2])

        psA = e_(nc.psum_tensor("psA", [128, 1024], F32))
        psB = e_(nc.psum_tensor("psB", [128, 1024], F32))
        psC = e_(nc.psum_tensor("psC", [128, 1024], F32))
        psD = e_(nc.psum_tensor("psD", [128, 1024], F32))

        class Sem:
            def __init__(self, name):
                self.h = e_(nc.semaphore(name))
                self.n = 0

        SN = ("s_const s_cin0 s_cin1 s_castD s_castA s_cout0 s_cout1 s_wsmm "
              "s_pe s_row s_stat s_bcm "
              "s_xld0 s_xld1 s_sq s_xbf s_xn s_wv s_wqk s_qkmm s_qkev s_vmm s_vev "
              "s_scm s_exp s_msk s_pvmm s_pvev s_scr0 s_scr1 s_rb0 s_rb1 "
              "s_x2 s_gmm s_f1 "
              "s_hrdy s_acc s_cwt s_bcw s_w13a s_w13b s_w2a s_w2b "
              "s_pa s_pg s_peo s_sil s_mul "
              "s_drn s_bcc s_out s_bar").split()
        sm = type("SM", (), {n: Sem(n) for n in SN})
        s_cin = [sm.s_cin0, sm.s_cin1]
        s_cinp = Sem("s_cinp")
        s_cout = [sm.s_cout0, sm.s_cout1]
        s_xld = [sm.s_xld0, sm.s_xld1]
        s_scr = [sm.s_scr0, sm.s_scr1]
        s_rb = [sm.s_rb0, sm.s_rb1]
        s_w13 = [sm.s_w13a, sm.s_w13b]
        s_w2 = [sm.s_w2a, sm.s_w2b]

        block = e_(nc.Block())
        ops = {q: [] for q in ("pe", "act", "dve", "sync", "pool")}

        selfsem = {"dve": Sem("self_dve"), "act": Sem("self_act")}
        last_done = {"dve": None, "act": None}

        def wt(q, s, v=None):
            v = s.n if v is None else v
            ops[q].append(lambda eng, h=s.h, vv=v: eng.wait_ge(h, vv))

        def em(q, meth, *args, inc=None, amt=1, chain=True, **kw):
            if q in ("dve", "act"):
                if chain and last_done[q] is not None:
                    ls, lv = last_done[q]
                    wt(q, ls, lv)
                if inc is None:
                    inc = selfsem[q]
                    amt = 1
            if inc is not None:
                inc.n += amt
            def f(eng, meth=meth, args=args, kw=kw, inc=inc, amt=amt):
                i = getattr(eng, meth)(*args, **kw)
                if inc is not None:
                    i.then_inc(inc.h, amt)
            ops[q].append(f)
            if q in ("dve", "act"):
                last_done[q] = (inc, inc.n)

        class Ring:
            def __init__(self, nslots):
                self.g = [[] for _ in range(nslots)]

            def acq(self, q, slot):
                for s, v in self.g[slot]:
                    wt(q, s, v)
                self.g[slot] = []

            def rel(self, slot, s, v=None):
                self.g[slot].append((s, s.n if v is None else v))

        pr = Ring(8)
        xr = Ring(2)

        wqk_r = Ring(1)
        E_r = Ring(6)
        sil_r = Ring(2)
        out_r = Ring(2)
        xat_r = Ring(2)
        w13_r = Ring(2)
        w2_r = Ring(2)
        cv_i = Ring(2)
        cv_o = Ring(2)

        def barrier():
            for q in ops:
                em(q, "nop", inc=sm.s_bar)
            v = sm.s_bar.n
            for q in ops:
                wt(q, sm.s_bar, v)
            last_done["dve"] = (sm.s_bar, v)
            last_done["act"] = (sm.s_bar, v)

        PS = {"A0": psA[:, 0:512], "A1": psA[:, 512:1024],
              "B0": psB[:, 0:512], "B1": psB[:, 512:1024],
              "C0": psC[:, 0:512], "C1": psC[:, 512:1024],
              "D0": psD[:, 0:512], "D1": psD[:, 512:1024]}
        BI = {"A0": 0, "A1": 1, "B0": 2, "B1": 3, "C0": 4, "C1": 5, "D0": 6, "D1": 7}

        def xn_sl(dc, t0, w=TBS):
            return A_xnh[:, dc * 2048 + t0: dc * 2048 + t0 + w]

        def x2_sl(dc, t0, w=TBS):
            return x2acc[:, dc * 2048 + t0: dc * 2048 + t0 + w]

        RW = [rows[0:1, 512 * k:512 * (k + 1)] for k in range(4)]

        em("sync", "dma_start", out=idn_sb[:, :], in_=idn_in, inc=sm.s_const, amt=16)
        em("sync", "dma_start", out=onc[:, :], in_=onc_in, inc=sm.s_const, amt=16)
        em("sync", "dma_start", out=oncb[:, :], in_=oncb_in, inc=sm.s_const, amt=16)
        em("sync", "dma_start", out=onr[:, :], in_=onr_in, inc=sm.s_const, amt=16)
        em("sync", "dma_start", out=mask_sb[:, :], in_=mask_in, inc=sm.s_const, amt=16)
        em("sync", "dma_start", out=rms_c[:, :], in_=rms_t, inc=sm.s_const, amt=16)
        em("sync", "dma_start", out=gw_sb[:, :].rearrange("p (c e) -> p c e", c=NDC),
           in_=gw_t.rearrange("c p e -> p c e"), inc=sm.s_const, amt=16)
        em("sync", "dma_start", out=epsc[:, :], in_=eps_in, inc=sm.s_const, amt=16)
        n_const = sm.s_const.n
        wt("act", sm.s_const, n_const)
        wt("dve", sm.s_const, n_const)

        cin = [x2acc[:, 0:4096], x2acc[:, 4096:8192]]
        cout = [A_xnh[:, 0:4096], A_xnh[:, 4096:8192]]

        wt("sync", sm.s_const, n_const)
        for i in range(8):
            cv_i.acq("sync", i % 2)
            em("sync", "dma_start", out=cin[i % 2][:, 0:1024],
               in_=wout2d[128 * i:128 * (i + 1), :], inc=s_cin[i % 2], amt=16)
            sld = s_cin[i % 2].n
            wt("pe", s_cin[i % 2], sld)
            if i == 0:
                pr.acq("pe", BI["D0"]); pr.acq("pe", BI["D1"])
            em("pe", "matmul", psD[0:1, 0:512], onc[:, :], cin[i % 2][:, 0:512],
               start=(i == 0), stop=(i == 7), skip_group_check=True)
            em("pe", "matmul", psD[0:1, 512:1024], onc[:, :], cin[i % 2][:, 512:1024],
               start=(i == 0), stop=(i == 7), skip_group_check=True, inc=sm.s_wsmm)
            cv_i.rel(i % 2, sm.s_wsmm)
        wt("dve", sm.s_wsmm, sm.s_wsmm.n)
        em("dve", "tensor_copy", rows[0:1, 0:512], psD[0:1, 0:512])
        em("dve", "tensor_copy", rows[0:1, 512:1024], psD[0:1, 512:1024], inc=sm.s_row)
        pr.rel(BI["D0"], sm.s_row); pr.rel(BI["D1"], sm.s_row)
        wsrow = sm.s_row.n
        for k in range(8):
            wt("pe", sm.s_row, wsrow + k)
            if k == 0:
                wt("pe", sm.s_const, n_const)
                pr.acq("pe", BI["B0"])
            em("pe", "transpose", psB[:, 0:1], rows[0:1, 128 * k:128 * (k + 1)],
               idn_sb[0:1, 0:1], inc=sm.s_pe)
            wt("dve", sm.s_pe, sm.s_pe.n)
            em("dve", "tensor_copy", wsum_c[:, k:k + 1], psB[:, 0:1], inc=sm.s_row)
        wt("dve", sm.s_const, n_const)
        for dc in range(NDC):
            em("dve", "tensor_scalar", gwp[:, 8 * dc:8 * dc + 8], gw_sb[:, 8 * dc:8 * dc + 8],
               rms_c[:, dc:dc + 1], None, ALU.mult,
               inc=(sm.s_row if dc == 7 else None))
        wt("pe", sm.s_row, sm.s_row.n)
        for dc in range(NDC):
            em("pe", "matmul", psB[0:8, 0:1], gwp[:, 8 * dc:8 * dc + 8], onc[:, :],
               start=(dc == 0), stop=(dc == 7),
               inc=(sm.s_pe if dc == 7 else None))
        wt("dve", sm.s_pe, sm.s_pe.n)
        em("dve", "tensor_copy", s_sb[:, :], psB[0:8, 0:1], inc=sm.s_row)
        wt("pe", sm.s_row, sm.s_row.n)
        em("pe", "transpose", psB[0:1, 0:8], s_sb[:, :], idn_sb[0:8, 0:8], inc=sm.s_pe)
        wt("dve", sm.s_pe, sm.s_pe.n)
        em("dve", "tensor_copy", srow[:, :], psB[0:1, 0:8], inc=sm.s_row)
        wt("pe", sm.s_row, sm.s_row.n)
        em("pe", "matmul", psB[:, 0:8], onr[:, :], srow[:, :], start=True, stop=True,
           inc=sm.s_pe)
        wt("dve", sm.s_pe, sm.s_pe.n)
        em("dve", "tensor_copy", sbc[:, :], psB[:, 0:8], inc=sm.s_row)
        pr.rel(BI["B0"], sm.s_row)

        tiles = []
        for e in range(NE):
            for ib in range(NIBLK):
                tiles.append((w1_t[e, ib].rearrange("c p i -> p c i"),
                              w1_s[e, ib].rearrange("c p i -> p c i"), 2048, NDC))
                tiles.append((w3_t[e, ib].rearrange("c p i -> p c i"),
                              w3_s[e, ib].rearrange("c p i -> p c i"), 2048, NDC))
        for e in range(NE):
            for dc in range(NDC):
                tiles.append((w2_t[e, dc].rearrange("c p d -> p c d"),
                              w2_s[e, dc].rearrange("c p d -> p c d"), 3584, NIC))
        for n in range(NH // 2):
            tiles.append((wqk_t[n].rearrange("c p v -> p c v"),
                          wqk_s[n].rearrange("c p v -> p c v"), 2048, NDC))
        for dc in range(NDC):
            tiles.append((wv_t[dc], wv_s[dc], 1024, None))

        cast_sem = [sm.s_castD, sm.s_castA]
        cast_q = ["dve", "act"]
        cast_meth = ["tensor_copy", "copy"]
        for i, (src, dst, ncol, csplit) in enumerate(tiles):
            sl = i % 2
            cin_v = cin[sl][:, 0:ncol]
            cout_v = cout[sl][:, 0:ncol]
            if csplit is not None:
                cin_v = cin_v.rearrange("p (c v) -> p c v", c=csplit)
                cout_v = cout_v.rearrange("p (c v) -> p c v", c=csplit)
            inq = "sync" if sl == 0 else "pool"
            insem = s_cin[0] if sl == 0 else s_cinp
            cv_i.acq(inq, sl)
            em(inq, "dma_start", out=cin_v, in_=src, inc=insem, amt=16)
            sld = insem.n
            q = cast_q[sl]
            wt(q, insem, sld)
            cv_o.acq(q, sl)
            em(q, cast_meth[sl], cout[sl][:, 0:ncol], cin[sl][:, 0:ncol],
               inc=cast_sem[sl])
            cv_i.rel(sl, cast_sem[sl])
            scast = cast_sem[sl].n
            outq = "pool" if sl == 0 else "sync"
            wt(outq, cast_sem[sl], scast)
            em(outq, "dma_start", out=dst, in_=cout_v, inc=s_cout[sl], amt=16)
            cv_o.rel(sl, s_cout[sl])

        for q in ("sync", "pool"):
            wt(q, s_cout[0], s_cout[0].n)
            wt(q, s_cout[1], s_cout[1].n)
        barrier()

        jg = 0
        Qg = 0
        exp_hist = {}
        pvmm_hist = {}
        pvev_hist = {}
        scr_hist = {}
        vev_hist = {}
        slab_used = {}
        w2_used = {}
        gi_sil = {}
        gi_mul = {}
        x2_hist = {}

        for b in range(NB):
            em("sync", "dma_start",
               out=A_wsm[:, 0:8192].rearrange("p (c v) -> p c v", c=NDC),
               in_=wv_s.rearrange("c p v -> p c v"), inc=sm.s_wv, amt=16)
            snap_wv = sm.s_wv.n
            em("pool", "memset", A_vact[:, :], 1.0, inc=sm.s_msk)
            snap_msv = sm.s_msk.n

            snap_xn = {}
            a1_state = {}

            def a1_pass1(tb):
                t0 = tb * TBS
                rb = ("D0", "D1") if tb % 2 == 0 else ("A0", "A1")
                psR = psD if tb % 2 == 0 else psA
                for dc in range(NDC):
                    i = tb * NDC + dc
                    sl = i % 2
                    xr.acq("sync", sl)
                    em("sync", "dma_start", out=xring[:, 512 * sl:512 * (sl + 1)],
                       in_=x_fm[b, dc * 128:(dc + 1) * 128, t0:t0 + TBS],
                       inc=s_xld[sl], amt=16)
                    sxl = s_xld[sl].n
                    wt("act", s_xld[sl], sxl)
                    E_r.acq("act", sl)
                    em("act", "activation", xsqr[:, 512 * sl:512 * (sl + 1)],
                       xring[:, 512 * sl:512 * (sl + 1)], AF.Square, inc=sm.s_sq,
                       chain=False)
                    ssq = sm.s_sq.n
                    wt("dve", s_xld[sl], sxl)
                    em("dve", "tensor_copy", xn_sl(dc, t0),
                       xring[:, 512 * sl:512 * (sl + 1)], inc=sm.s_xbf, chain=False)
                    sbf = sm.s_xbf.n
                    xr.rel(sl, sm.s_sq, ssq)
                    xr.rel(sl, sm.s_xbf, sbf)
                    wt("pe", sm.s_xbf, sbf)
                    wt("pe", sm.s_sq, ssq)
                    if dc == 0:
                        pr.acq("pe", BI[rb[0]]); pr.acq("pe", BI[rb[1]])
                    em("pe", "matmul", psR[0:1, 0:512], oncb[:, :],
                       xn_sl(dc, t0),
                       start=(dc == 0), stop=(dc == 7), skip_group_check=True)
                    em("pe", "matmul", psR[0:1, 512:1024], oncb[:, :],
                       xsqr[:, 512 * sl:512 * (sl + 1)],
                       start=(dc == 0), stop=(dc == 7), skip_group_check=True,
                       inc=sm.s_stat)
                    E_r.rel(sl, sm.s_stat)
                a1_state[tb] = sm.s_stat.n

            def a1_pass2(tb):
                t0 = tb * TBS
                rb = ("D0", "D1") if tb % 2 == 0 else ("A0", "A1")
                psR = psD if tb % 2 == 0 else psA
                bb = ("B0", "B1") if tb % 2 == 0 else ("C0", "C1")
                psBC = psB if tb % 2 == 0 else psC
                wt("dve", sm.s_stat, a1_state[tb])
                wt("dve", sm.s_bcm, sm.s_bcm.n)
                em("dve", "tensor_scalar", RW[0], psR[0:1, 0:512], 1.0 / 1024, None, ALU.mult)
                em("dve", "tensor_scalar", RW[1], psR[0:1, 512:1024], 1.0 / 1024, None, ALU.mult)
                em("dve", "tensor_tensor", RW[2], RW[0], RW[0], ALU.mult)
                em("dve", "tensor_tensor", RW[1], RW[1], RW[2], ALU.subtract, inc=sm.s_row)
                pr.rel(BI[rb[0]], sm.s_row); pr.rel(BI[rb[1]], sm.s_row)
                wt("act", sm.s_row, sm.s_row.n)
                em("act", "activation", RW[2], RW[1], AF.Sqrt, bias=1e-5, inc=sm.s_sil)
                wt("dve", sm.s_sil, sm.s_sil.n)
                em("dve", "reciprocal", RW[1], RW[2])
                em("dve", "tensor_tensor", RW[2], RW[0], RW[1], ALU.mult)
                em("dve", "tensor_scalar", RW[2], RW[2], -1.0, None, ALU.mult, inc=sm.s_row)
                snap_row = sm.s_row.n
                wt("pe", sm.s_row, snap_row)
                pr.acq("pe", BI[bb[0]]); pr.acq("pe", BI[bb[1]])
                em("pe", "matmul", psBC[:, 0:512], onr[:, :], RW[1], start=True, stop=True)
                em("pe", "matmul", psBC[:, 512:1024], onr[:, :], RW[2], start=True, stop=True,
                   inc=sm.s_bcm)
                snap_bc = sm.s_bcm.n
                for dc in range(NDC):
                    if dc == 0:
                        wt("dve", sm.s_bcm, snap_bc)
                    em("dve", "tensor_tensor", tmp2[:, :],
                       xn_sl(dc, t0), psBC[:, 0:512], ALU.mult)
                    em("dve", "tensor_tensor", xn_sl(dc, t0), tmp2[:, :],
                       psBC[:, 512:1024], ALU.add, inc=sm.s_xn)
                snap_xn[tb] = sm.s_xn.n
                pr.rel(BI[bb[0]], sm.s_xn); pr.rel(BI[bb[1]], sm.s_xn)

            a1_pass1(0)
            for tb in range(1, NTB):
                a1_pass1(tb)
                a1_pass2(tb - 1)
            a1_pass2(NTB - 1)

            wt("pe", sm.s_msk, snap_msv)
            wt("pe", sm.s_wv, snap_wv)
            for tc in range(NTC):
                tb = tc // 4
                bank = ["C0", "C1"] if tc % 2 == 0 else ["D0", "D1"]
                psv = psC if tc % 2 == 0 else psD
                wt("pe", sm.s_xn, snap_xn[tb])
                pr.acq("pe", BI[bank[0]]); pr.acq("pe", BI[bank[1]])
                for dc in range(NDC):
                    lhs = A_xnh[:, dc * 2048 + tc * 128: dc * 2048 + (tc + 1) * 128]
                    em("pe", "matmul", psv[:, 0:512], lhs,
                       A_wsm[:, dc * 1024: dc * 1024 + 512],
                       start=(dc == 0), stop=(dc == 7), skip_group_check=True)
                    em("pe", "matmul", psv[:, 512:1024], lhs,
                       A_wsm[:, dc * 1024 + 512: (dc + 1) * 1024],
                       start=(dc == 0), stop=(dc == 7), skip_group_check=True,
                       inc=(sm.s_vmm if dc == 7 else None))
                svm = sm.s_vmm.n
                wt("dve", sm.s_vmm, svm)
                if tc == 0:
                    wt("dve", sm.s_msk, snap_msv)
                for half in range(2):
                    dst = A_vact[:, tc * 1040 + half * 520: tc * 1040 + half * 520 + 520]
                    dst = dst.rearrange("p (n x) -> p n x", n=8)[:, :, 0:64]
                    src = psv[:, half * 512:(half + 1) * 512].rearrange("p (n h) -> p n h", n=8)
                    em("dve", "tensor_copy", dst, src,
                       inc=(sm.s_vev if half == 1 else None))
                vev_hist[tc] = sm.s_vev.n
                pr.rel(BI[bank[0]], sm.s_vev); pr.rel(BI[bank[1]], sm.s_vev)
            snap_vall = sm.s_vev.n

            for hp in range(NH // 2):
                wqk_r.acq("sync", 0)
                em("sync", "dma_start",
                   out=wqkh[:, 0:2048].rearrange("p (c v) -> p c v", c=NDC),
                   in_=wqk_s[hp].rearrange("c p v -> p c v"), inc=sm.s_wqk, amt=16)
                snap_wqk = sm.s_wqk.n
                wt("pe", sm.s_wqk, snap_wqk)
                for tb in range(NTB):
                    wt("pe", sm.s_xn, snap_xn[tb])
                    for bank, dst, coff in (("D0", q_sb, 0), ("D1", k_sb, 128)):
                        pr.acq("pe", BI[bank])
                        for dc in range(NDC):
                            em("pe", "matmul", PS[bank],
                               wqkh[:, dc * 256 + coff: dc * 256 + coff + 128],
                               A_xnh[:, dc * 2048 + tb * 512: dc * 2048 + (tb + 1) * 512],
                               start=(dc == 0), stop=(dc == 7), skip_group_check=True,
                               inc=(sm.s_qkmm if dc == 7 else None))
                        sqm = sm.s_qkmm.n
                        wt("dve", sm.s_qkmm, sqm)
                        if hp > 0 and tb == 0:
                            wt("dve", sm.s_scm, sm.s_scm.n)
                        em("dve", "tensor_copy", dst[:, tb * 512:(tb + 1) * 512],
                           PS[bank], inc=sm.s_qkev)
                        pr.rel(BI[bank], sm.s_qkev)
                snap_qkev = sm.s_qkev.n
                wqk_r.rel(0, sm.s_qkmm)
                wt("pe", sm.s_qkev, snap_qkev)
                for sub in range(2):
                    n = 2 * hp + sub
                    pbase = 64 * sub
                    LAG = 5
                    SBK = ("A0", "A1", "C0", "C1", "D0", "D1")
                    js = [(Q, j) for Q in range(4) for j in range(4 * Q + 4)]
                    info = []
                    for idx in range(len(js) + LAG):
                        if idx < len(js):
                            Q, j = js[idx]
                            diag = j >= 4 * Q
                            sb = jg % 6
                            sbank = SBK[sb]
                            if jg >= 6:
                                wt("pe", sm.s_exp, exp_hist[jg - 6])
                            pr.acq("pe", BI[sbank])
                            em("pe", "matmul", PS[sbank],
                               k_sb[pbase:pbase + 64, j * 128:(j + 1) * 128],
                               q_sb[pbase:pbase + 64, Q * 512:(Q + 1) * 512],
                               start=True, stop=True, skip_group_check=True,
                               inc=sm.s_scm)
                            sscm = sm.s_scm.n
                            wt("act", sm.s_scm, sscm)
                            if jg >= 6:
                                wt("act", sm.s_pvmm, pvmm_hist[jg - 6])
                            E_r.acq("act", sb)
                            em("act", "activation", E_sb[:, sb * 512:(sb + 1) * 512],
                               PS[sbank], AF.Exp, scale=0.125, inc=sm.s_exp,
                               chain=False)
                            exp_hist[jg] = sm.s_exp.n
                            pr.rel(BI[sbank], sm.s_exp)
                            mskv = None
                            if diag:
                                c = j - 4 * Q
                                off = 384 - 128 * c
                                wt("dve", sm.s_exp, exp_hist[jg])
                                em("dve", "tensor_tensor",
                                   E_sb[:, sb * 512:(sb + 1) * 512],
                                   E_sb[:, sb * 512:(sb + 1) * 512],
                                   mask_sb[:, off:off + 512], ALU.mult, inc=sm.s_msk,
                                   chain=False)
                                mskv = sm.s_msk.n
                            info.append((Q, j, sb, mskv, exp_hist[jg], jg))
                            jg += 1
                        if idx >= LAG:
                            Q, j, sb, mskv, expv, jgi = info[idx - LAG]
                            jmax = 4 * Q + 4
                            pvbank = "B0" if Q % 2 == 0 else "B1"
                            pvb = psB[:, (Q % 2) * 512:(Q % 2) * 512 + 260]
                            if j == 0:
                                pr.acq("pe", BI[pvbank])
                            if mskv is not None:
                                wt("pe", sm.s_msk, mskv)
                            else:
                                wt("pe", sm.s_exp, expv)
                            for qt in range(4):
                                em("pe", "matmul", pvb[:, qt * 65:qt * 65 + 65],
                                   E_sb[:, sb * 512 + qt * 128: sb * 512 + (qt + 1) * 128],
                                   A_vact[:, j * 1040 + n * 65: j * 1040 + (n + 1) * 65],
                                   start=(j == 0 and qt == 0),
                                   stop=(j == jmax - 1 and qt == 3),
                                   skip_group_check=True, inc=sm.s_pvmm)
                            pvmm_hist[jgi] = sm.s_pvmm.n
                            E_r.rel(sb, sm.s_pvmm)
                            if j == jmax - 1:
                                spv = sm.s_pvmm.n
                                wt("dve", sm.s_pvmm, spv)
                                oQ = (Q % 2) * 256
                                out_r.acq("dve", Q % 2)
                                zv = pvb.rearrange("p (q x) -> p q x", q=4)[:, :, 64:65]
                                em("dve", "reciprocal",
                                   rz_sb[:, (Q % 2) * 4:(Q % 2) * 4 + 4].rearrange(
                                       "p (q x) -> p q x", q=4), zv)
                                for qt in range(4):
                                    em("dve", "tensor_scalar",
                                       outsb[:, oQ + qt * 64: oQ + (qt + 1) * 64],
                                       pvb[:, qt * 65:qt * 65 + 64],
                                       rz_sb[:, (Q % 2) * 4 + qt:(Q % 2) * 4 + qt + 1],
                                       None, ALU.mult, inc=sm.s_pvev)
                                pvev_hist[Qg] = sm.s_pvev.n
                                pr.rel(BI[pvbank], sm.s_pvev)
                                wt("pool", sm.s_pvev, sm.s_pvev.n)
                                em("pool", "dma_start",
                                   out=scr[n, Q * 512:(Q + 1) * 512, :].rearrange(
                                       "(q p) h -> p q h", p=128),
                                   in_=outsb[:, oQ:oQ + 256].rearrange(
                                       "p (q h) -> p q h", q=4),
                                   inc=s_scr[Q % 2], amt=16)
                                out_r.rel(Q % 2, s_scr[Q % 2])
                                Qg += 1
            snap_exp_end = sm.s_exp.n
            w13_r.g[0].append((sm.s_qkmm, sm.s_qkmm.n))
            w13_r.g[1].append((sm.s_qkmm, sm.s_qkmm.n))

            wt("pool", s_scr[0], s_scr[0].n)
            wt("pool", s_scr[1], s_scr[1].n)
            scr_done = (s_scr[0].n, s_scr[1].n)
            w2_r.g[0] = [(sm.s_vmm, sm.s_vmm.n)]
            w2_r.g[1] = [(sm.s_vmm, sm.s_vmm.n)]
            x2tb = {}

            def emit_D(tb):
                for dc in range(NDC):
                    i = dc * NTB + tb
                    sl4 = i % 2
                    sl2 = i % 2
                    xat_r.acq("sync", sl4)
                    wt("sync", s_scr[0], scr_done[0])
                    wt("sync", s_scr[1], scr_done[1])
                    em("sync", "dma_start", out=xattn[:, sl4 * 512:(sl4 + 1) * 512],
                       in_=scr[2 * dc:2 * dc + 2].rearrange(
                           "n (q r) h -> (n q) (r h)", q=64)[:, tb * 512:(tb + 1) * 512],
                       inc=s_rb[sl4], amt=16)
                    srb = s_rb[sl4].n
                    xr.acq("sync", sl2)
                    em("sync", "dma_start", out=xring[:, 512 * sl2:512 * (sl2 + 1)],
                       in_=x_fm[b, dc * 128:(dc + 1) * 128, tb * 512:(tb + 1) * 512],
                       inc=s_xld[sl2], amt=16)
                    sxl = s_xld[sl2].n
                    wt("dve", s_rb[sl4], srb)
                    wt("dve", s_xld[sl2], sxl)
                    em("dve", "scalar_tensor_tensor", x2_sl(dc, tb * 512),
                       xattn[:, sl4 * 512:(sl4 + 1) * 512], wsum_c[:, dc:dc + 1],
                       xring[:, 512 * sl2:512 * (sl2 + 1)], ALU.mult, ALU.add,
                       inc=sm.s_x2)
                    xr.rel(sl2, sm.s_x2)
                    xat_r.rel(sl4, sm.s_x2)
                x2tb[tb] = sm.s_x2.n

            for tb in range(NTB):
                t0 = tb * TBS
                emit_D(tb)
                for dc in range(NDC):
                    i = tb * NDC + dc
                    sl = i % 2
                    wt("act", sm.s_x2, x2tb[tb])
                    E_r.acq("act", sl)
                    em("act", "activation", xsqr[:, 512 * sl:512 * (sl + 1)],
                       x2_sl(dc, t0), AF.Square, inc=sm.s_sq)
                    ssq = sm.s_sq.n
                    wt("dve", sm.s_x2, x2tb[tb])
                    sil_r.acq("dve", sl)
                    em("dve", "tensor_copy", xbfr[:, 512 * sl:512 * (sl + 1)],
                       x2_sl(dc, t0), inc=sm.s_xbf)
                    sbf = sm.s_xbf.n
                    wt("pe", sm.s_xbf, sbf)
                    wt("pe", sm.s_sq, ssq)
                    if dc == 0:
                        pr.acq("pe", BI["D0"]); pr.acq("pe", BI["D1"])
                    em("pe", "matmul", psD[0:1, 0:512], oncb[:, :],
                       xbfr[:, 512 * sl:512 * (sl + 1)],
                       start=(dc == 0), stop=(dc == 7), skip_group_check=True)
                    em("pe", "matmul", psD[0:1, 512:1024], oncb[:, :],
                       xsqr[:, 512 * sl:512 * (sl + 1)],
                       start=(dc == 0), stop=(dc == 7), skip_group_check=True,
                       inc=sm.s_stat)
                    E_r.rel(sl, sm.s_stat)
                    sil_r.rel(sl, sm.s_stat)
                snap_stat = sm.s_stat.n
                wt("pe", sm.s_x2, x2tb[tb])
                pr.acq("pe", BI["A0"])
                for dc in range(NDC):
                    em("pe", "matmul", psA[0:8, 0:512], gwp[:, 8 * dc:8 * dc + 8],
                       x2_sl(dc, t0), start=(dc == 0), stop=(dc == 7),
                       inc=(sm.s_gmm if dc == 7 else None))
                snap_gmm = sm.s_gmm.n
                wt("dve", sm.s_gmm, snap_gmm)
                em("dve", "tensor_copy", gl8, psA[0:8, 0:512], inc=sm.s_f1)
                pr.rel(BI["A0"], sm.s_f1)
                wt("dve", sm.s_stat, snap_stat)
                wt("dve", sm.s_bcm, sm.s_bcm.n)
                wt("dve", sm.s_pe, sm.s_pe.n)
                em("dve", "tensor_scalar", RW[0], psD[0:1, 0:512], 1.0 / 1024, None, ALU.mult)
                em("dve", "tensor_scalar", RW[1], psD[0:1, 512:1024], 1.0 / 1024, None, ALU.mult)
                em("dve", "tensor_tensor", RW[2], RW[0], RW[0], ALU.mult)
                em("dve", "tensor_tensor", RW[1], RW[1], RW[2], ALU.subtract, inc=sm.s_row)
                pr.rel(BI["D0"], sm.s_row); pr.rel(BI["D1"], sm.s_row)
                wt("act", sm.s_row, sm.s_row.n)
                em("act", "activation", RW[2], RW[1], AF.Sqrt, bias=1e-5, inc=sm.s_sil)
                wt("dve", sm.s_sil, sm.s_sil.n)
                em("dve", "reciprocal", RW[2], RW[2])
                em("dve", "tensor_tensor", RW[3], RW[1], RW[2], ALU.mult)
                em("dve", "tensor_tensor", RW[3], RW[3], RW[2], ALU.mult, inc=sm.s_row)
                wt("act", sm.s_row, sm.s_row.n)
                em("act", "activation", RW[1], RW[3], AF.Sqrt, bias=1e-6, inc=sm.s_sil)
                wt("dve", sm.s_sil, sm.s_sil.n)
                em("dve", "reciprocal", RW[1], RW[1])
                em("dve", "tensor_tensor", RW[3], RW[2], RW[1], ALU.mult)
                em("dve", "tensor_tensor", RW[0], RW[0], RW[2], ALU.mult)
                em("dve", "tensor_scalar", RW[0], RW[0], -1.0, None, ALU.mult)
                em("dve", "tensor_tensor", RW[1], RW[0], RW[1], ALU.mult, inc=sm.s_row)
                snap_row = sm.s_row.n
                wt("pe", sm.s_row, snap_row)
                for bk in ("B0", "B1", "C0", "C1"):
                    pr.acq("pe", BI[bk])
                em("pe", "matmul", psB[:, 0:512], onr[:, :], RW[2], start=True, stop=True)
                em("pe", "matmul", psB[:, 512:1024], onr[:, :], RW[0], start=True, stop=True)
                em("pe", "matmul", psC[:, 0:512], onr[:, :], RW[3], start=True, stop=True)
                em("pe", "matmul", psC[:, 512:1024], onr[:, :], RW[1], start=True, stop=True,
                   inc=sm.s_bcm)
                wt("act", sm.s_bcm, sm.s_bcm.n)
                em("act", "activation", psB[:, 0:512], psB[:, 0:512], AF.Copy, bias=1.0,
                   inc=sm.s_sil)
                snap_r1 = sm.s_sil.n
                wt("pe", sm.s_f1, sm.s_f1.n)
                pr.acq("pe", BI["D1"])
                for qt in range(4):
                    em("pe", "transpose", psD[:, 512 + qt * 8:520 + qt * 8],
                       gl8f[0:8, qt * 128:(qt + 1) * 128], idn_sb[0:8, 0:8])
                for qt in range(4):
                    em("pe", "transpose", psD[:, 544 + qt:545 + qt],
                       RW[3][0:1, qt * 128:(qt + 1) * 128], idn_sb[0:1, 0:1])
                for qt in range(4):
                    em("pe", "transpose", psD[:, 548 + qt:549 + qt],
                       RW[1][0:1, qt * 128:(qt + 1) * 128], idn_sb[0:1, 0:1],
                       inc=(sm.s_pe if qt == 3 else None))
                snap_tr = sm.s_pe.n
                BCS = (128, 4, 8)
                def c3(a_, b_):
                    return rtv[:, a_:b_].rearrange("p (q e) -> p q e", q=4)
                def c1(a_):
                    return rtv[:, a_:a_ + 4].rearrange("p (q x) -> p q x", q=4)
                wt("dve", sm.s_pe, snap_tr)
                em("dve", "tensor_copy", rtv[:, 0:40], psD[:, 512:552])
                pr.rel(BI["D1"], selfsem["dve"])
                sbc3 = sbc[:, :].rearrange("p (x e) -> p x e", x=1).broadcast_to(BCS)
                em("dve", "tensor_tensor", c3(40, 72), sbc3,
                   c1(36).broadcast_to(BCS), ALU.mult)
                em("dve", "tensor_tensor", c3(72, 104), c3(0, 32),
                   c1(32).broadcast_to(BCS), ALU.mult)
                em("dve", "tensor_tensor", c3(72, 104), c3(72, 104), c3(40, 72), ALU.add)
                em("dve", "tensor_reduce", c1(40), c3(72, 104), axis=AX, op=ALU.max)
                em("dve", "tensor_tensor", c3(72, 104), c3(72, 104),
                   c1(40).broadcast_to(BCS), ALU.subtract, inc=sm.s_f1)
                wt("act", sm.s_f1, sm.s_f1.n)
                em("act", "activation", rtv[:, 104:136], rtv[:, 72:104], AF.Exp,
                   inc=sm.s_sil)
                wt("dve", sm.s_sil, sm.s_sil.n)
                em("dve", "tensor_reduce", c1(44), c3(104, 136), axis=AX, op=ALU.add)
                em("dve", "reciprocal", rtv[:, 48:52], rtv[:, 44:48])
                em("dve", "tensor_tensor", c3(136, 168), c3(104, 136),
                   c1(48).broadcast_to(BCS), ALU.mult)
                em("dve", "tensor_reduce", c1(52), c3(136, 168), axis=AX, op=ALU.max)
                em("dve", "tensor_tensor", c3(168, 200), c3(136, 168),
                   c1(52).broadcast_to(BCS), ALU.is_equal)
                em("dve", "scalar_tensor_tensor", rtv[:, 200:232], rtv[:, 168:200],
                   -2.0, rtv[:, 136:168], ALU.mult, ALU.add)
                em("dve", "tensor_reduce", c1(56), c3(200, 232), axis=AX, op=ALU.max)
                em("dve", "tensor_tensor", rtv[:, 60:64], rtv[:, 52:56],
                   rtv[:, 56:60], ALU.add)
                em("dve", "reciprocal", rtv[:, 64:68], rtv[:, 60:64])
                em("dve", "tensor_tensor", rtv[:, 68:72], rtv[:, 52:56],
                   rtv[:, 64:68], ALU.mult)
                em("dve", "tensor_tensor", rtv[:, 232:236], rtv[:, 56:60],
                   rtv[:, 64:68], ALU.mult)
                em("dve", "tensor_tensor", c3(104, 136), c3(200, 232),
                   c1(56).broadcast_to(BCS), ALU.is_equal)
                em("dve", "tensor_tensor", c3(72, 104), c3(168, 200),
                   c1(68).broadcast_to(BCS), ALU.mult)
                em("dve", "tensor_tensor", c3(136, 168), c3(104, 136),
                   c1(232).broadcast_to(BCS), ALU.mult)
                em("dve", "tensor_tensor", cw_all[:, :], rtv[:, 72:104],
                   rtv[:, 136:168], ALU.add, inc=sm.s_cwt)
                snap_cwt = sm.s_cwt.n
                wt("dve", sm.s_sil, snap_r1)
                wt("dve", sm.s_gmm, snap_gmm)
                for dc in range(NDC):
                    em("dve", "tensor_tensor", tmp2[:, :], x2_sl(dc, t0),
                       psC[:, 0:512], ALU.mult)
                    em("dve", "tensor_tensor", tmp2[:, :], tmp2[:, :],
                       psC[:, 512:1024], ALU.add)
                    em("dve", "tensor_scalar", xn_sl(dc, t0), tmp2[:, :],
                       rms_c[:, dc:dc + 1], None, ALU.mult, inc=sm.s_hrdy)
                    em("dve", "tensor_tensor", tmp2[:, :], x2_sl(dc, t0),
                       psB[:, 0:512], ALU.mult)
                    em("dve", "tensor_tensor", x2_sl(dc, t0), tmp2[:, :],
                       psB[:, 512:1024], ALU.add, inc=sm.s_acc)
                snap_h = sm.s_hrdy.n
                snap_acc = sm.s_acc.n
                pr.rel(BI["B0"], sm.s_acc); pr.rel(BI["B1"], sm.s_acc)
                pr.rel(BI["C0"], sm.s_hrdy); pr.rel(BI["C1"], sm.s_hrdy)


                def build_cw_row(et):
                    sl = et % 2
                    wt("pe", sm.s_cwt, snap_cwt)
                    if et >= 2:
                        wt("pe", sm.s_bcw, cw_slot_user[sl])
                    pr.acq("pe", BI["D1"])
                    for qt in range(4):
                        em("pe", "transpose", psD[0:1, 512 + qt * 128:640 + qt * 128],
                           cw_all[:, qt * 8 + et: qt * 8 + et + 1], idn_sb[:, :],
                           inc=(sm.s_bcw if qt == 3 else None))
                    tsnap = sm.s_bcw.n
                    wt("dve", sm.s_bcw, tsnap)
                    for qt in range(4):
                        em("dve", "tensor_copy", cwT_e[0:1, sl * 512 + qt * 128:
                                                       sl * 512 + (qt + 1) * 128],
                           psD[0:1, 512 + qt * 128:640 + qt * 128],
                           inc=(sm.s_cwt if qt == 3 else None))
                    pr.rel(BI["D1"], sm.s_cwt)
                    cw_built[et] = sm.s_cwt.n

                cw_built = {}
                cw_slot_user = {0: 0, 1: 0}
                build_cw_row(0)
                for e in range(NE):
                    bcbank = "D0" if e % 2 == 0 else "D1"
                    wt("pe", sm.s_cwt, cw_built[e])
                    pr.acq("pe", BI[bcbank])
                    em("pe", "matmul", PS[bcbank], onr[:, :],
                       cwT_e[0:1, (e % 2) * 512:(e % 2) * 512 + 512],
                       start=True, stop=True, inc=sm.s_bcw)
                    cw_slot_user[e % 2] = sm.s_bcw.n
                    wt("dve", sm.s_bcw, sm.s_bcw.n)
                    em("dve", "tensor_copy", bc_sb[:, :], PS[bcbank], inc=sm.s_bcc)
                    pr.rel(BI[bcbank], sm.s_bcc)
                    if e + 1 < NE:
                        build_cw_row(e + 1)

                    for ib in range(NIBLK):
                        k = (s_w13[0].n + s_w13[1].n) // 32
                        wb = k % 2
                        w13_r.acq("sync", wb)
                        em("sync", "dma_start",
                           out=A_w13[:, wb * 2048:(wb + 1) * 2048].rearrange(
                               "p (c v) -> p c v", c=NDC),
                           in_=w1_s[e, ib].rearrange("c p i -> p c i"),
                           inc=s_w13[wb], amt=16)
                        em("sync", "dma_start",
                           out=A_w13[:, 4096 + wb * 2048:4096 + (wb + 1) * 2048].rearrange(
                               "p (c v) -> p c v", c=NDC),
                           in_=w3_s[e, ib].rearrange("c p i -> p c i"),
                           inc=s_w13[wb], amt=16)
                        sw13 = s_w13[wb].n
                        wt("pe", s_w13[wb], sw13)
                        if ib == 0:
                            wt("pe", sm.s_hrdy, snap_h)
                            if e == 0:
                                wt("pe", sm.s_acc, snap_acc)
                                wt("pe", sm.s_exp, snap_exp_end)
                            else:
                                wt("pe", sm.s_peo, sm.s_peo.n)
                        for ic in range(2):
                            gidx = sm.s_pg.n
                            pb = gidx % 2
                            abank = "A0" if pb == 0 else "A1"
                            gbank = "B0" if pb == 0 else "B1"
                            pr.acq("pe", BI[abank]); pr.acq("pe", BI[gbank])
                            for dc in range(NDC):
                                em("pe", "matmul", PS[abank],
                                   A_w13[:, wb * 2048 + dc * 256 + ic * 128:
                                         wb * 2048 + dc * 256 + (ic + 1) * 128],
                                   xn_sl(dc, t0),
                                   start=(dc == 0), stop=(dc == 7),
                                   skip_group_check=True,
                                   inc=(sm.s_pa if dc == 7 else None))
                            spa = sm.s_pa.n
                            for dc in range(NDC):
                                em("pe", "matmul", PS[gbank],
                                   A_w13[:, 4096 + wb * 2048 + dc * 256 + ic * 128:
                                         4096 + wb * 2048 + dc * 256 + (ic + 1) * 128],
                                   xn_sl(dc, t0),
                                   start=(dc == 0), stop=(dc == 7),
                                   skip_group_check=True,
                                   inc=(sm.s_pg if dc == 7 else None))
                            spg = sm.s_pg.n
                            wt("act", sm.s_pa, spa)
                            sil_r.acq("act", gidx % 2)
                            em("act", "activation",
                               sil_sb[:, (gidx % 2) * 512:(gidx % 2 + 1) * 512],
                               PS[abank], AF.Silu, inc=sm.s_sil)
                            gi_sil[gidx] = sm.s_sil.n
                            pr.rel(BI[abank], sm.s_sil)
                            gl = gidx % 28
                            wt("dve", sm.s_sil, gi_sil[gidx])
                            wt("dve", sm.s_pg, spg)
                            em("dve", "tensor_tensor",
                               A_vact[:, gl * 512:(gl + 1) * 512],
                               sil_sb[:, (gidx % 2) * 512:(gidx % 2 + 1) * 512],
                               PS[gbank], ALU.mult, inc=sm.s_mul)
                            gi_mul[gidx] = sm.s_mul.n
                            sil_r.rel(gidx % 2, sm.s_mul)
                            pr.rel(BI[gbank], sm.s_mul)
                        w13_r.rel(wb, sm.s_pg)
                    snap_mul_all = sm.s_mul.n
                    wt("pe", sm.s_mul, snap_mul_all)
                    for dc in range(NDC):
                        slw2 = (s_w2[0].n + s_w2[1].n) // 16
                        wb2 = slw2 % 2
                        w2_r.acq("pool", wb2)
                        em("pool", "dma_start",
                           out=A_wsm[:, wb2 * 3584:(wb2 + 1) * 3584].rearrange(
                               "p (c v) -> p c v", c=NIC),
                           in_=w2_s[e, dc].rearrange("c p d -> p c d"),
                           inc=s_w2[wb2], amt=16)
                        sw2 = s_w2[wb2].n
                        wt("pe", s_w2[wb2], sw2)
                        pbo = sm.s_peo.n % 2
                        obank = "C0" if pbo == 0 else "C1"
                        pr.acq("pe", BI[obank])
                        for ic2 in range(NIC):
                            em("pe", "matmul", PS[obank],
                               A_wsm[:, wb2 * 3584 + ic2 * 128:wb2 * 3584 + (ic2 + 1) * 128],
                               A_vact[:, ic2 * 512:(ic2 + 1) * 512],
                               start=(ic2 == 0), stop=(ic2 == NIC - 1),
                               skip_group_check=True,
                               inc=(sm.s_peo if ic2 == NIC - 1 else None))
                        speo = sm.s_peo.n
                        w2_r.rel(wb2, sm.s_peo)
                        wt("dve", sm.s_peo, speo)
                        em("dve", "tensor_tensor", tmp2[:, :], PS[obank],
                           bc_sb[:, :], ALU.mult)
                        em("dve", "tensor_tensor", x2_sl(dc, t0), x2_sl(dc, t0),
                           tmp2[:, :], ALU.add, inc=sm.s_drn)
                        pr.rel(BI[obank], sm.s_drn)
                snap_tb_done = sm.s_drn.n
                wt("sync", sm.s_drn, snap_tb_done)
                for dc in range(NDC):
                    em("sync", "dma_start",
                       out=out_fm[b, dc * 128:(dc + 1) * 128, t0:t0 + TBS],
                       in_=x2_sl(dc, t0), inc=sm.s_out, amt=16)
            wt("sync", sm.s_out, sm.s_out.n)
            barrier()

        wt("sync", sm.s_out, sm.s_out.n)

        @block.tensor
        def _(eng):
            for f in ops["pe"]:
                f(eng)

        @block.scalar
        def _(eng):
            for f in ops["act"]:
                f(eng)

        @block.vector
        def _(eng):
            for f in ops["dve"]:
                f(eng)

        @block.sync
        def _(eng):
            for f in ops["sync"]:
                f(eng)

        @block.gpsimd
        def _(eng):
            for f in ops["pool"]:
                f(eng)

    return nc


def kernel(L0_attn_x, L0_attn_W_qkv, L0_attn_W_out, L0_ffn_rms_weight,
           L0_ffn_gate_w, L0_ffn_w1, L0_ffn_w2, L0_ffn_w3):
    x = np.asarray(L0_attn_x, dtype=np.float32)
    W_qkv = np.asarray(L0_attn_W_qkv, dtype=np.float32)
    W_out = np.asarray(L0_attn_W_out, dtype=np.float32)
    rms = np.asarray(L0_ffn_rms_weight, dtype=np.float32)
    gw = np.asarray(L0_ffn_gate_w, dtype=np.float32)
    w1 = np.asarray(L0_ffn_w1, dtype=np.float32)
    w2 = np.asarray(L0_ffn_w2, dtype=np.float32)
    w3 = np.asarray(L0_ffn_w3, dtype=np.float32)

    qh = W_qkv[:, :, 0:64]
    kh = W_qkv[:, :, 64:128]
    qkp = np.stack([qh.reshape(8, 2, D, 64), kh.reshape(8, 2, D, 64)], axis=1)
    wqk_tl = np.ascontiguousarray(
        qkp.transpose(0, 3, 1, 2, 4).reshape(NH // 2, NDC, 128, 256))
    wv_tl = np.ascontiguousarray(
        W_qkv[:, :, 128:192].transpose(1, 0, 2).reshape(NDC, 128, NH * HD))
    wout2d = np.ascontiguousarray(W_out.reshape(D, D))
    rms_tl = np.ascontiguousarray(rms.reshape(NDC, 128).T)
    gw_tl = np.ascontiguousarray(gw.T.reshape(NDC, 128, NE))
    w1_tl = np.ascontiguousarray(
        w1.transpose(0, 2, 1).reshape(NE, NDC, 128, NIBLK, 256).transpose(0, 3, 1, 2, 4))
    w3_tl = np.ascontiguousarray(
        w3.transpose(0, 2, 1).reshape(NE, NDC, 128, NIBLK, 256).transpose(0, 3, 1, 2, 4))
    w2_tl = np.ascontiguousarray(
        w2.transpose(0, 2, 1).reshape(NE, NIC, 128, NDC, 128).transpose(0, 3, 1, 2, 4))
    idn = np.eye(128, dtype=np.float32)
    onc_np = np.ones((128, 1), dtype=np.float32)
    oncb_np = np.ones((128, 1), dtype=ml_dtypes.bfloat16)
    onr_np = np.ones((1, 128), dtype=np.float32)
    mask_np = (np.arange(896)[None, :] - np.arange(128)[:, None] >= 384).astype(
        ml_dtypes.bfloat16)
    eps_np = np.tile(np.array([[1e-5, 1e-6]], dtype=np.float32), (128, 1))

    nc = build_nc()
    in_maps = []
    for c in range(8):
        xs = np.ascontiguousarray(x[4 * c:4 * c + 4].transpose(0, 2, 1))
        in_maps.append({
            "x_fm": xs, "wqk_t": wqk_tl, "wv_t": wv_tl, "wout2d": wout2d,
            "rms_t": rms_tl, "gw_t": gw_tl, "w1_t": w1_tl, "w3_t": w3_tl,
            "w2_t": w2_tl, "idn": idn, "onc": onc_np, "oncb": oncb_np,
            "onr": onr_np, "mask": mask_np, "epsc": eps_np,
        })
    res = run_bass_kernel_spmd(nc, in_maps, core_ids=list(range(8)))
    outs = [np.ascontiguousarray(r["out_fm"].transpose(0, 2, 1)) for r in res.results]
    return np.concatenate(outs, axis=0).astype(np.float32)
